# version 14
# baseline (speedup 1.0000x reference)
import numpy as np
import ml_dtypes
import concourse.bass as bass
import concourse.mybir as mybir
from concourse.bass_utils import run_bass_kernel_spmd

F32 = mybir.dt.float32
BF16 = mybir.dt.bfloat16
AF = mybir.ActivationFunctionType
ALU = mybir.AluOpType

D = 2048
S = 2048
TOWN = 1024
NQ, NKV, HD = 32, 8, 64
H = 8192
SHIFT = -20.0

ENGINES = ("t", "a", "v", "g", "s")
SEMS = ("t", "a", "v", "g0", "g1", "g2", "g3", "s0", "s1", "s2", "s3")


class Sched:

    def __init__(self, nc):
        self.nc = nc
        self.sems = {}
        self.cnt = {s: 0 for s in SEMS}
        self.maxw = {}
        self.dma_idx = {"g": 0, "s": 0}
        self.last_dma_tok = {"g": {}, "s": {}}

    def open(self):
        for s in SEMS:
            self.sems[s] = self.nc.alloc_semaphore(f"sem_{s}")

    def begin_phase(self):
        self.ops = {e: [] for e in ENGINES}

    def add(self, e, fn, waits=(), inc=False, inc_n=1):
        if e in ("g", "s") and inc:
            semkey = e + str(self.dma_idx[e] % 4)
            self.dma_idx[e] += 1
        else:
            semkey = e
        w = []
        for t in waits:
            if not t:
                continue
            sk, v = t
            if v <= 0:
                continue
            if sk == e and e in ("t", "a", "v"):
                continue
            w.append((sk, v))
        self.ops[e].append((w, fn, inc, inc_n, semkey))
        if inc:
            self.cnt[semkey] += inc_n
            if e in ("g", "s"):
                self.last_dma_tok[e][semkey] = (semkey, self.cnt[semkey])
        return (semkey, self.cnt.get(semkey, 0))

    def dma_tokens(self, e):
        return list(self.last_dma_tok[e].values())

    def run(self, block):
        def runner(e):
            def go(eng):
                prev_was_dma = [False]
                for waits, fn, inc, inc_n, semkey in self.ops[e]:
                    emitted = False
                    for (sk, v) in waits:
                        key = (e, sk)
                        if self.maxw.get(key, 0) < v:
                            eng.wait_ge(self.sems[sk], v)
                            self.maxw[key] = v
                            emitted = True
                    is_dma = inc and e in ("g", "s")
                    if is_dma and prev_was_dma[0] and not emitted:
                        eng.wait_ge(self.sems[semkey],
                                    self.maxw.get((e, semkey), 0))
                    ins = fn(eng)
                    if inc:
                        ins.then_inc(self.sems[semkey], inc_n)
                    prev_was_dma[0] = is_dma
            return go

        block.tensor(runner("t"))
        block.scalar(runner("a"))
        block.vector(runner("v"))
        block.gpsimd(runner("g"))
        block.sync(runner("s"))


def build_program():
    nc = bass.Bass("TRN2", target_bir_lowering=False)

    xfull = nc.dram_tensor("x", [D, S], F32, kind="ExternalInput").ap()
    wqkv = nc.dram_tensor("wqkv", [D, 3072], F32, kind="ExternalInput").ap()
    bqkv = nc.dram_tensor("bqkv", [3072], F32, kind="ExternalInput").ap()
    wo = nc.dram_tensor("wo", [D, D], F32, kind="ExternalInput").ap()
    bo = nc.dram_tensor("bo", [D], F32, kind="ExternalInput").ap()
    upt = nc.dram_tensor("upt", [D, H], F32, kind="ExternalInput").ap()
    dnt = nc.dram_tensor("dnt", [H, D], F32, kind="ExternalInput").ap()
    us = nc.dram_tensor("us", [H], F32, kind="ExternalInput").ap()
    ds = nc.dram_tensor("ds", [D], F32, kind="ExternalInput").ap()
    osc = nc.dram_tensor("osc", [1], F32, kind="ExternalInput").ap()
    out = nc.dram_tensor("out", [D, TOWN], F32, kind="ExternalOutput").ap()

    xfull_r = xfull.rearrange("(c p) t -> p c t", p=128)
    wqkv_r = wqkv.rearrange("(c p) f -> p c f", p=128)
    wo_r = wo.rearrange("(c p) e -> p c e", p=128)
    upt_r = upt.rearrange("(c p) h -> p c h", p=128)
    dnt_r = dnt.rearrange("(c p) f -> p c f", p=128)

    sch = Sched(nc)
    sch.open()

    from contextlib import ExitStack
    with ExitStack() as _ctx:
        _e = _ctx.enter_context
        ones_bf = _e(nc.sbuf_tensor("ones_bf", [128, 1], BF16))
        onesf = _e(nc.sbuf_tensor("onesf", [128, 128], F32))
        onesb2 = _e(nc.sbuf_tensor("onesb2", [128, 128], BF16))
        c0125 = _e(nc.sbuf_tensor("c0125", [128, 1], F32))
        cn20 = _e(nc.sbuf_tensor("cn20", [128, 1], F32))
        ceps = _e(nc.sbuf_tensor("ceps", [128, 1], F32))
        cinvn = _e(nc.sbuf_tensor("cinvn", [128, 1], F32))
        bq_sb = _e(nc.sbuf_tensor("bq_sb", [128, 24], F32))
        qbias = _e(nc.sbuf_tensor("qbias", [128, 16], F32))
        bo_sb = _e(nc.sbuf_tensor("bo_sb", [128, 16], F32))
        us_sb = _e(nc.sbuf_tensor("us_sb", [128, 64], F32))
        ds_sb = _e(nc.sbuf_tensor("ds_sb", [128, 16], F32))
        dsos = _e(nc.sbuf_tensor("dsos", [128, 16], F32))
        os_sb = _e(nc.sbuf_tensor("os_sb", [1, 1], F32))
        os_col = _e(nc.sbuf_tensor("os_col", [128, 1], F32))
        vb_bc = _e(nc.sbuf_tensor("vb_bc", [128, 512], BF16))
        A1 = _e(nc.sbuf_tensor("A1", [128, 16 * TOWN], BF16))
        BE = _e(nc.sbuf_tensor("BE", [128, 16 * TOWN], BF16))
        Cxn = _e(nc.sbuf_tensor("Cxn", [128, 16 * TOWN], BF16))
        A2 = _e(nc.sbuf_tensor("A2", [128, 4 * S + 16 * 520], BF16))
        WS = _e(nc.sbuf_tensor("WS", [128, 16384], BF16))
        Hb = _e(nc.sbuf_tensor("Hb", [128, 4096], BF16))

        Ksb = A2[:, 0:4 * S]
        Vaug_off = 4 * S
        W1 = WS[:, 0:8192]
        W2 = WS[:, 8192:16384]
        ering = BE[:, 0:8192]

        ev_xl = {}
        pre_toks = []
        sch.begin_phase()
        with nc.sbuf_tensor("vb_row", [1, 512], BF16) as vb_row, \
             nc.psum_tensor("p0a", [128, 1], F32) as p0a, \
             nc.psum_tensor("p0b", [128, 512], F32) as p0b, \
             nc.Block("blk_p0") as blk:
            def f(eng):
                eng.memset(ones_bf[:, :], 1.0)
                eng.memset(onesf[:, :], 1.0)
                eng.memset(onesb2[:, :], 1.0)
                eng.memset(c0125[:, :], 0.125)
                eng.memset(cn20[:, :], SHIFT)
                eng.memset(ceps[:, :], 1e-5)
                return eng.memset(cinvn[:, :], 1.0 / D)
            ev_ms = sch.add("v", f, inc=True)

            for dst, srcfn in (
                (bq_sb, lambda: bqkv.rearrange("(a b) -> b a", b=128)),
                (bo_sb, lambda: bo.rearrange("(a b) -> b a", b=128)),
                (us_sb, lambda: us.rearrange("(a b) -> b a", b=128)),
                (ds_sb, lambda: ds.rearrange("(a b) -> b a", b=128)),
                (os_sb, lambda: osc.rearrange("(a b) -> a b", a=1)),
            ):
                def f(eng, dst=dst, srcfn=srcfn):
                    return eng.dma_start(out=dst[:, :], in_=srcfn(),
                                         allow_slow_non_contiguous=True)
                sch.add("s", f, inc=True, inc_n=16)
            def f(eng):
                return eng.dma_start(out=vb_row[:, :],
                                     in_=bqkv[2560:3072].rearrange("(a b) -> a b", a=1),
                                     allow_slow_non_contiguous=True)
            ev_vbl = sch.add("g", f, inc=True, inc_n=16)
            for pc in range(2):
                def f(eng, pc=pc):
                    return eng.dma_start(
                        out=A1.ap().rearrange("p (c t) -> p c t", t=TOWN)[:, pc * 8:(pc + 1) * 8, :],
                        in_=xfull_r[:, pc * 8:(pc + 1) * 8, 0:TOWN])
                ev_xl[(0, pc)] = sch.add("g", f, inc=True, inc_n=16)
            def f(eng):
                return eng.dma_start(
                    out=W1.rearrange("p (c j) -> p c j", j=512),
                    in_=wqkv_r[:, :, 0:512])
            pre_toks.append(sch.add("g", f, inc=True, inc_n=16))
            for pc in range(2):
                def f(eng, pc=pc):
                    return eng.dma_start(
                        out=BE.ap().rearrange("p (c t) -> p c t", t=TOWN)[:, pc * 8:(pc + 1) * 8, :],
                        in_=xfull_r[:, pc * 8:(pc + 1) * 8, TOWN:S])
                ev_xl[(1, pc)] = sch.add("g", f, inc=True, inc_n=16)
            def f(eng):
                return eng.dma_start(
                    out=W2.rearrange("p (c j) -> p c j", j=512),
                    in_=wqkv_r[:, :, 512:1024])
            pre_toks.append(sch.add("g", f, inc=True, inc_n=16))

            def f(eng):
                eng.matmul(p0a[:, :], onesf[0:1, :], os_sb[0:1, 0:1], start=True, stop=True)
                return eng.matmul(p0b[:, :], onesb2[0:1, :], vb_row[0:1, :], start=True, stop=True)
            ev_bc = sch.add("t", f, waits=[ev_ms, ev_vbl] + sch.dma_tokens("s"), inc=True)

            def f(eng):
                eng.tensor_copy(os_col[:, :], p0a[:, :])
                return eng.tensor_copy(vb_bc[:, :], p0b[:, :])
            ev_cp = sch.add("v", f, waits=[ev_bc], inc=True)

            def f(eng):
                eng.mul(dsos[:, :], ds_sb[:, :], os_col[:, 0:1])
                return eng.mul(qbias[:, :], bq_sb[:, 0:16], c0125[:, 0:1])
            sch.add("a", f, waits=[ev_cp] + sch.dma_tokens("s"), inc=True)
            sch.run(blk)

        ev_aff = {}
        ev_bcv_h1 = None
        ev_sqmm = {}
        ev_stats_prev = [None]
        with nc.sbuf_tensor("rows_f", [1, 3072], F32) as rows_f, \
             nc.sbuf_tensor("rows_b", [1, 2048], BF16) as rows_b, \
             nc.sbuf_tensor("Ff", [128, 2048], BF16) as Ff:
            r_mean = rows_f[0:1, 0:1024]
            r_var = rows_f[0:1, 1024:2048]
            r_tmp = rows_f[0:1, 2048:3072]
            r_rstd = rows_b[0:1, 0:1024]
            r_b = rows_b[0:1, 1024:2048]

            def emit_ln_stats(half, xsrc, ps_s, ps_q):
                hb_off = half * 2048
                for fc in range(16):
                    gfc = half * 16 + fc
                    def f(eng, fc=fc, xsrc=xsrc, ps_s=ps_s):
                        for n in range(2):
                            ins = eng.matmul(
                                ps_s[0:1, n * 512:(n + 1) * 512], ones_bf[:, :],
                                xsrc[:, fc * TOWN + n * 512: fc * TOWN + (n + 1) * 512],
                                start=(fc == 0), stop=(fc == 15))
                        return ins
                    sch.add("t", f, waits=[ev_xl[(half, fc // 8)], ev_stats_prev[0]], inc=True)
                    def f(eng, fc=fc, xsrc=xsrc, hb_off=hb_off):
                        return eng.activation(
                            Hb[:, hb_off + (fc % 2) * 1024: hb_off + (fc % 2) * 1024 + 1024],
                            xsrc[:, fc * TOWN:(fc + 1) * TOWN], AF.Square)
                    ev_sq = sch.add("a", f,
                                    waits=[ev_xl[(half, fc // 8)],
                                           ev_sqmm.get(gfc - 2)],
                                    inc=True)
                    def f(eng, fc=fc, hb_off=hb_off, ps_q=ps_q):
                        for n in range(2):
                            ins = eng.matmul(
                                ps_q[0:1, n * 512:(n + 1) * 512], ones_bf[:, :],
                                Hb[:, hb_off + (fc % 2) * 1024 + n * 512:
                                   hb_off + (fc % 2) * 1024 + (n + 1) * 512],
                                start=(fc == 0), stop=(fc == 15))
                        return ins
                    ev_sqmm[gfc] = sch.add("t", f,
                                           waits=[ev_sq, ev_stats_prev[0]],
                                           inc=True)
                def f(eng, ps_s=ps_s, ps_q=ps_q):
                    eng.tensor_scalar(r_mean, ps_s[0:1, :], cinvn[0:1, 0:1], None, op0=ALU.mult)
                    eng.tensor_scalar(r_var, ps_q[0:1, :], cinvn[0:1, 0:1], None, op0=ALU.mult)
                    eng.drain()
                    eng.tensor_tensor(r_tmp, r_mean, r_mean, op=ALU.mult)
                    eng.drain()
                    return eng.tensor_tensor(r_var, r_var, r_tmp, op=ALU.subtract)
                ev_st = sch.add("v", f, waits=[ev_sqmm[half * 16 + 15]], inc=True)
                ev_stats_prev[0] = ev_st
                def f(eng):
                    eng.activation(r_tmp, r_var, AF.Ln, bias=ceps[0:1, 0:1])
                    eng.drain()
                    return eng.activation(r_rstd, r_tmp, AF.Exp, scale=-0.5)
                ev_rs = sch.add("a", f, waits=[ev_st], inc=True)
                def f(eng):
                    return eng.scalar_tensor_tensor(
                        r_b, r_mean, -1.0, r_rstd, op0=ALU.mult, op1=ALU.mult)
                return sch.add("v", f, waits=[ev_rs], inc=True)

            def emit_ln_affine(half, ev_bb, xsrc, xn_dst, ps_b, ps_c):
                ev_bcv = None
                for n in range(2):
                    def f(eng, n=n, ps_b=ps_b, ps_c=ps_c):
                        eng.matmul(ps_b[:, :], onesb2[0:1, :],
                                   r_rstd[0:1, n * 512:(n + 1) * 512],
                                   start=True, stop=True)
                        return eng.matmul(ps_c[:, :], onesb2[0:1, :],
                                          r_b[0:1, n * 512:(n + 1) * 512],
                                          start=True, stop=True)
                    ev_bct0 = sch.add("t", f, waits=[ev_bb, ev_bcv], inc=True)
                    def f(eng, n=n, ps_b=ps_b, ps_c=ps_c):
                        eng.tensor_copy(Ff[:, n * 512:(n + 1) * 512], ps_b[:, :])
                        return eng.tensor_copy(Ff[:, 1024 + n * 512:1024 + (n + 1) * 512],
                                               ps_c[:, :])
                    ev_bcv = sch.add("v", f, waits=[ev_bct0], inc=True)
                for fc in range(16):
                    def f(eng, fc=fc, xsrc=xsrc, xn_dst=xn_dst):
                        if fc == 0:
                            eng.drain()
                        d = xn_dst[:, fc * TOWN:(fc + 1) * TOWN]
                        return eng.tensor_tensor(d, xsrc[:, fc * TOWN:(fc + 1) * TOWN],
                                                 Ff[:, 0:1024], op=ALU.mult)
                    sch.add("v", f, waits=[ev_bcv], inc=True)
                for fc in range(16):
                    gfc = half * 16 + fc
                    def f(eng, fc=fc, xn_dst=xn_dst):
                        d = xn_dst[:, fc * TOWN:(fc + 1) * TOWN]
                        return eng.tensor_tensor(d, d, Ff[:, 1024:2048], op=ALU.add)
                    ev_aff[gfc] = sch.add("v", f, inc=True)
                return ev_bcv

            sch.begin_phase()
            with nc.psum_tensor("p1s", [1, 1024], F32) as p1s, \
                 nc.psum_tensor("p1q", [1, 1024], F32) as p1q, \
                 nc.psum_tensor("p1b", [128, 512], F32) as p1b, \
                 nc.psum_tensor("p1c", [128, 512], F32) as p1c, \
                 nc.Block("blk_p1") as blk:
                ev_vm = sch.add("v", lambda eng: eng.memset(A2[:, Vaug_off:], 1.0), inc=True)
                ev_bb0 = emit_ln_stats(0, A1, p1s, p1q)
                emit_ln_affine(0, ev_bb0, A1, Cxn, p1b, p1c)
                sch.run(blk)

        sch.begin_phase()
        with nc.sbuf_tensor("Wa", [128, 8192], BF16) as Wa, \
             nc.sbuf_tensor("Wb", [128, 8192], BF16) as Wb, \
             nc.sbuf_tensor("pvstage", [128, 1024], F32) as pvstage, \
             nc.psum_tensor("R0", [128, 1024], F32) as R0, \
             nc.psum_tensor("R1", [128, 1024], F32) as R1, \
             nc.psum_tensor("PVF", [128, 2048], F32) as PVF, \
             nc.Block("blk_mega") as blk:
            PV0 = PVF[:, 0:512]
            PV1 = PVF[:, 512:1024]
            FF = (PVF[:, 1024:1536], PVF[:, 1536:2048])
            regs = (R0, R1)
            ao_tmp = BE[0:64, 12288:13312]
            bc_sb = BE[:, 13312:13824]
            bc_sb2 = BE[:, 13824:14336]
            recb = BE[:, 13312:14336]

            class FStream:
                def __init__(self):
                    self.seq = []
                    for hb in range(16):
                        for hc in range(4):
                            for tcx in range(2):
                                self.seq.append(("u", hb, hc, tcx))
                        for df in range(16):
                            for tcx in range(2):
                                self.seq.append(("d", hb, df, tcx))
                    self.pos = 0
                    self.n = 0
                    self.ev_evac = {}
                    self.ev_upg = {}
                    self.ev_dng = {}
                    self.ev_signu = {}
                    self.ev_signd = {}
                    self.ev_relu = {}
                    self.ev_sacc = {}
                    self.slab_u = set()
                    self.slab_d = set()
                    self.ddma = {}
                    self.udma = {}

                def _emit_uraw(self, hb):
                    for half in (0, 1):
                        def f(eng, hb=hb, half=half):
                            return eng.dma_start(
                                out=Wa.ap().rearrange("p (c j) -> p c j", j=512)[:, half * 8:(half + 1) * 8, :],
                                in_=upt_r[:, half * 8:(half + 1) * 8, hb * 512:(hb + 1) * 512])
                        w = [self.ev_upg.get(hb - 1)]
                        if hb == 0:
                            w += [ev_aff[31], ev_bcv_h1]
                        self.udma[(hb, half)] = sch.add("g", f, waits=w,
                                                        inc=True, inc_n=16)

                def _emit_signu_chunk(self, hb, q):
                    def fs(eng, q=q):
                        reg = Wa[:, q * 2048:(q + 1) * 2048]
                        return eng.sign(reg, reg)
                    self.ev_signu[hb] = sch.add(
                        "a", fs, waits=[self.udma[(hb, q // 2)]], inc=True)

                def _emit_slab_u(self, hb):
                    self.slab_u.add(hb)
                    self._emit_uraw(hb)
                    for q in range(4):
                        self._emit_signu_chunk(hb, q)

                def _emit_slab_d(self, hb):
                    self.slab_d.add(hb)
                    for half in (0, 1):
                        def f(eng, hb=hb, half=half):
                            return eng.dma_start(
                                out=Wb.ap().rearrange("p (c j) -> p c j", j=2048)[:, half * 2:(half + 1) * 2, :],
                                in_=dnt_r[:, hb * 4 + half * 2: hb * 4 + half * 2 + 2, :])
                        w = [self.ev_dng.get(hb - 1)]
                        if hb == 0:
                            w += [ev_aff[31], ev_bcv_h1]
                        self.ddma[(hb, half)] = sch.add("g", f, waits=w,
                                                        inc=True, inc_n=16)

                def _emit_signd_chunk(self, hb, q):
                    def fs(eng, q=q):
                        reg = Wb[:, q * 2048:(q + 1) * 2048]
                        return eng.sign(reg, reg)
                    self.ev_signd[hb] = sch.add(
                        "a", fs, waits=[self.ddma[(hb, q // 2)]], inc=True)

                def emit(self, k, pred=None):
                    c = 0
                    while c < k and self.pos < len(self.seq):
                        it = self.seq[self.pos]
                        if pred is not None and not pred(it):
                            break
                        kind, hb, x, tcx = it
                        n = self.n
                        ff = FF[n % 2]
                        if kind == "u":
                            if hb not in self.slab_u:
                                self._emit_slab_u(hb)
                            if hb not in self.slab_d:
                                self._emit_slab_d(hb)
                            w = [self.ev_signu[hb], self.ev_evac.get(n - 2)]
                            if hb == 0:
                                w.append(ev_aff[15])
                            def f(eng, hc=x, tcx=tcx, ff=ff):
                                for c_ in range(16):
                                    ins = eng.matmul(
                                        ff[:, :],
                                        Wa[:, c_ * 512 + hc * 128: c_ * 512 + hc * 128 + 128],
                                        Cxn[:, c_ * TOWN + tcx * 512: c_ * TOWN + tcx * 512 + 512],
                                        start=(c_ == 0), stop=(c_ == 15))
                                return ins
                            tok = sch.add("t", f, waits=w, inc=True)
                            self.ev_upg[hb] = tok
                            def fr(eng, hb=hb, hc=x, tcx=tcx, ff=ff):
                                return eng.activation(
                                    Hb[:, hc * 1024 + tcx * 512: hc * 1024 + tcx * 512 + 512],
                                    ff[:, :], AF.Relu,
                                    scale=us_sb[:, hb * 4 + hc: hb * 4 + hc + 1])
                            rt = sch.add("a", fr, waits=[tok, self.ev_dng.get(hb - 1)], inc=True)
                            self.ev_relu[(hb, tcx)] = rt
                            self.ev_evac[n] = rt
                            gidx = x * 2 + tcx
                            if 3 <= gidx <= 6:
                                self._emit_signd_chunk(hb, gidx - 3)
                        else:
                            didx = x * 2 + tcx
                            if didx == 0 and hb + 1 < 16 and hb + 1 not in self.slab_u:
                                self.slab_u.add(hb + 1)
                                self._emit_uraw(hb + 1)
                            if didx in (24, 26, 28, 30) and hb + 1 < 16:
                                self._emit_signu_chunk(hb + 1, (didx - 24) // 2)
                            w = [self.ev_signd[hb], self.ev_relu[(hb, tcx)],
                                 self.ev_evac.get(n - 2)]
                            def f(eng, df=x, tcx=tcx, ff=ff):
                                for hc in range(4):
                                    ins = eng.matmul(
                                        ff[:, :],
                                        Wb[:, hc * 2048 + df * 128: hc * 2048 + df * 128 + 128],
                                        Hb[:, hc * 1024 + tcx * 512: hc * 1024 + tcx * 512 + 512],
                                        start=(hc == 0), stop=(hc == 3))
                                return ins
                            tok = sch.add("t", f, waits=w, inc=True)
                            self.ev_dng[hb] = tok
                            def fv(eng, hb=hb, df=x, tcx=tcx, ff=ff):
                                dst = WS[:, df * 1024 + tcx * 512: df * 1024 + tcx * 512 + 512]
                                if hb == 0:
                                    return eng.tensor_copy(dst, ff[:, :])
                                return eng.tensor_tensor(dst, dst, ff[:, :], op=ALU.add)
                            vt = sch.add("v", fv, waits=[tok], inc=True)
                            self.ev_evac[n] = vt
                            self.ev_sacc[(hb, x, tcx)] = vt
                        self.n += 1
                        self.pos += 1
                        c += 1
                    return c

            ffn = FStream()
            ev_bb1 = None

            psq = [R0[:, 0:512], R0[:, 512:1024]]
            psv = [R1[:, 0:512], R1[:, 512:1024]]
            ev_mm_sl = {}
            ev_mm_first = {}
            ev_ev = {}
            ev_vevac = {}
            ev_dma_sl = {}
            ev_kevac_last = None
            qv_i = 0
            kv_i = 0
            for sl in range(6):
                if sl == 1:
                    ev_bb1 = emit_ln_stats(1, BE, PVF[0:1, 0:1024],
                                           PVF[0:1, 1024:2048])
                if sl == 2:
                    ev_bcv_h1 = emit_ln_affine(1, ev_bb1, BE, BE,
                                               PVF[:, 0:512], PVF[:, 512:1024])
                Wb_ = W1 if sl % 2 == 0 else W2
                if sl < 2:
                    ev_dma_sl[sl] = pre_toks[sl]
                else:
                    def f(eng, sl=sl, Wb_=Wb_):
                        return eng.dma_start(
                            out=Wb_.rearrange("p (c j) -> p c j", j=512),
                            in_=wqkv_r[:, :, sl * 512:(sl + 1) * 512])
                    ev_dma_sl[sl] = sch.add("g", f,
                                            waits=[ev_mm_sl.get(sl - 2),
                                                   ev_mm_first.get(sl - 1)],
                                            inc=True, inc_n=16)
                if sl < 4:
                    for fsub in range(4):
                        qf = sl * 4 + fsub
                        for tc in range(2):
                            ps = psq[qv_i % 2]
                            def f(eng, Wb_=Wb_, fsub=fsub, tc=tc, ps=ps):
                                for c in range(16):
                                    ins = eng.matmul(
                                        ps,
                                        Wb_[:, c * 512 + fsub * 128: c * 512 + fsub * 128 + 128],
                                        Cxn[:, c * TOWN + tc * 512: c * TOWN + tc * 512 + 512],
                                        start=(c == 0), stop=(c == 15))
                                return ins
                            ev_mm = sch.add(
                                "t", f,
                                waits=[ev_dma_sl[sl], ev_aff[15],
                                       ev_ev.get(("q", qv_i % 2))],
                                inc=True)
                            ev_mm_first.setdefault(sl, ev_mm)
                            ev_mm_sl[sl] = ev_mm
                            def f(eng, qf=qf, tc=tc, ps=ps):
                                return eng.activation(
                                    A1[:, qf * TOWN + tc * 512: qf * TOWN + tc * 512 + 512],
                                    ps, AF.Identity,
                                    bias=qbias[:, qf:qf + 1], scale=0.125)
                            ev_ev[("q", qv_i % 2)] = sch.add("a", f, waits=[ev_mm], inc=True)
                            qv_i += 1
                elif sl == 4:
                    for fsub in range(4):
                        for tcf in range(4):
                            ps = psq[qv_i % 2]
                            src = Cxn if tcf < 2 else BE
                            tcl = tcf % 2
                            hh = 0 if tcf < 2 else 1
                            def f(eng, Wb_=Wb_, fsub=fsub, src=src, tcl=tcl, ps=ps):
                                for c in range(16):
                                    ins = eng.matmul(
                                        ps,
                                        Wb_[:, c * 512 + fsub * 128: c * 512 + fsub * 128 + 128],
                                        src[:, c * TOWN + tcl * 512: c * TOWN + tcl * 512 + 512],
                                        start=(c == 0), stop=(c == 15))
                                return ins
                            ev_mm = sch.add(
                                "t", f,
                                waits=[ev_dma_sl[sl], ev_aff[hh * 16 + 15],
                                       ev_ev.get(("q", qv_i % 2))],
                                inc=True)
                            ev_mm_first.setdefault(sl, ev_mm)
                            ev_mm_sl[sl] = ev_mm
                            def f(eng, fsub=fsub, tcf=tcf, ps=ps):
                                return eng.activation(
                                    Ksb[:, fsub * S + tcf * 512: fsub * S + tcf * 512 + 512],
                                    ps, AF.Identity,
                                    bias=bq_sb[:, 16 + fsub:17 + fsub], scale=1.0)
                            ev_kevac_last = ev_ev[("q", qv_i % 2)] = \
                                sch.add("a", f, waits=[ev_mm], inc=True)
                            qv_i += 1
                else:
                    for tt in range(16):
                        ps = psv[kv_i % 2]
                        src = Cxn if tt < 8 else BE
                        ttl = tt % 8
                        hh = 0 if tt < 8 else 1
                        def f(eng, Wb_=Wb_, src=src, ttl=ttl, ps=ps):
                            for c in range(16):
                                ins = eng.matmul(
                                    ps,
                                    src[:, c * TOWN + ttl * 128: c * TOWN + ttl * 128 + 128],
                                    Wb_[:, c * 512: c * 512 + 512],
                                    start=(c == 0), stop=(c == 15))
                            return ins
                        ev_mm = sch.add(
                            "t", f,
                            waits=[ev_dma_sl[sl], ev_aff[hh * 16 + 15],
                                   ev_ev.get(("v", kv_i % 2)), ev_vm, ev_bcv_h1],
                            inc=True)
                        ev_mm_first.setdefault(sl, ev_mm)
                        ev_mm_sl[sl] = ev_mm
                        def f(eng, tt=tt, ps=ps):
                            o3 = A2[:, Vaug_off + tt * 520: Vaug_off + tt * 520 + 520]
                            return eng.tensor_tensor(
                                o3.rearrange("p (g e) -> p g e", e=65)[:, :, 0:64],
                                ps.rearrange("p (g e) -> p g e", e=64),
                                vb_bc[:, :].rearrange("p (g e) -> p g e", e=64),
                                op=ALU.add)
                        ev_vevac[tt] = ev_ev[("v", kv_i % 2)] = \
                            sch.add("v", f, waits=[ev_mm], inc=True)
                        kv_i += 1
                        if tt % 2 == 1:
                            ffn.emit(1, pred=lambda it: it[0] == "u" and it[1] == 0)

            ev_qk = {}
            ev_exp = {}
            ev_pv = {}
            ev_pvcopy = {}
            ev_recip = {}
            ev_bct = {}
            ev_bccopy = {}
            ev_mult = {}
            ev_aodma = {}

            def unit_params(P):
                c_ = P // 8
                i_ = (P % 8) // 2
                tc = P % 2
                return c_, i_, tc, 4 * c_ + i_, 2 * c_, 2 * c_ + 1

            def emit_recip(P):
                def frc(eng):
                    eng.activation(pvstage[64:65, 0:1024],
                                   pvstage[64:65, 0:1024], AF.Ln)
                    eng.drain()
                    return eng.activation(recb[64:65, 0:1024],
                                          pvstage[64:65, 0:1024], AF.Exp, scale=-1.0)
                ev_recip[P] = sch.add("a", frc,
                                      waits=[ev_pvcopy[P], ev_bct.get(P - 1)],
                                      inc=True)

            def emit_norm(P):
                def fb(eng):
                    eng.matmul(PV0[0:64, :], onesb2[64:65, 0:64],
                               recb[64:65, 0:512],
                               start=True, stop=True, tile_position=(64, 0))
                    return eng.matmul(PV1[0:64, :], onesb2[64:65, 0:64],
                                      recb[64:65, 512:1024],
                                      start=True, stop=True, tile_position=(64, 0))
                ev_bct[P] = sch.add("t", fb, waits=[ev_recip[P]], inc=True)
                def fc(eng):
                    eng.tensor_copy(bc_sb[0:64, :], PV0[0:64, :])
                    return eng.tensor_copy(bc_sb2[0:64, :], PV1[0:64, :])
                ev_bccopy[P] = sch.add("v", fc, waits=[ev_bct[P]], inc=True)

            def emit_mult(P):
                c_, i_, tc, qch, gA, gB = unit_params(P)
                slot = qch * TOWN + tc * 512
                def fm(eng, P=P, slot=slot):
                    eng.tensor_tensor(A1[0:64, slot:slot + 512],
                                      pvstage[0:64, 0:512], bc_sb[0:64, :],
                                      op=ALU.mult)
                    return eng.tensor_tensor(
                        ao_tmp[0:64, (P % 2) * 512:(P % 2) * 512 + 512],
                        pvstage[0:64, 512:1024], bc_sb2[0:64, :], op=ALU.mult)
                ev_mult[P] = sch.add("v", fm,
                                     waits=[ev_bccopy[P], ev_aodma.get(P - 2)],
                                     inc=True)
                def fd(eng, P=P, slot=slot):
                    return eng.dma_start(
                        out=A1[64:128, slot:slot + 512],
                        in_=ao_tmp[0:64, (P % 2) * 512:(P % 2) * 512 + 512])
                ev_aodma[P] = sch.add("s", fd, waits=[ev_mult[P]],
                                      inc=True, inc_n=16)

            def emit_pv(P, kt):
                c_, i_, tc, qch, gA, gB = unit_params(P)
                slot = (kt % 8) * 1024
                vb0 = Vaug_off + kt * 520 + gA * 65
                vb1 = Vaug_off + kt * 520 + gB * 65
                def fp(eng, kt=kt, slot=slot, vb0=vb0, vb1=vb1):
                    eng.matmul(PV0[0:65, :], A2[:, vb0:vb0 + 65],
                               ering[:, slot:slot + 512],
                               start=(kt == 0), stop=(kt == 15))
                    return eng.matmul(PV1[0:65, :], A2[:, vb1:vb1 + 65],
                                      ering[:, slot + 512:slot + 1024],
                                      start=(kt == 0), stop=(kt == 15))
                w = [ev_exp[(P, kt)]]
                if kt == 0:
                    w += [ev_pvcopy.get(P - 1), ev_bccopy.get(P - 1)]
                if P == 0:
                    w.append(ev_vevac[kt])
                ev_pv[(P, kt)] = sch.add("t", fp, waits=w, inc=True)

            for P in range(32):
                c_, i_, tc, qch, gA, gB = unit_params(P)
                for kt in range(16):
                    reg = regs[kt % 2]
                    def fq(eng, c_=c_, kt=kt, qch=qch, tc=tc, reg=reg):
                        eng.matmul(
                            reg[:, 0:512],
                            Ksb[0:64, c_ * S + kt * 128: c_ * S + kt * 128 + 128],
                            A1[0:64, qch * TOWN + tc * 512: qch * TOWN + tc * 512 + 512],
                            start=True, stop=True)
                        return eng.matmul(
                            reg[:, 512:1024],
                            Ksb[64:128, c_ * S + kt * 128: c_ * S + kt * 128 + 128],
                            A1[64:128, qch * TOWN + tc * 512: qch * TOWN + tc * 512 + 512],
                            start=True, stop=True, tile_position=(64, 0))
                    w = [ev_exp.get((P, kt - 2), ev_exp.get((P - 1, kt + 14)))]
                    if P == 0 and kt == 0:
                        w.append(ev_kevac_last)
                    if P == 0 and kt == 1:
                        w.append(ev_vevac[15])
                    ev_qk[(P, kt)] = sch.add("t", fq, waits=w, inc=True)
                    def fe(eng, kt=kt, reg=reg):
                        return eng.activation(
                            ering[:, (kt % 8) * 1024: (kt % 8) * 1024 + 1024],
                            reg[:, :], AF.Exp, bias=cn20[:, 0:1], scale=1.0)
                    ev_exp[(P, kt)] = sch.add(
                        "a", fe, waits=[ev_qk[(P, kt)],
                                        ev_pv.get((P, kt - 8),
                                                  ev_pv.get((P - 1, kt + 8)))],
                        inc=True)
                    if kt == 0 and P > 0:
                        emit_recip(P - 1)
                    if kt == 2 and P > 0:
                        emit_norm(P - 1)
                    if kt == 3 and P > 0:
                        emit_mult(P - 1)
                    if kt >= 4:
                        emit_pv(P, kt - 4)
                    ffn.emit(1)
                for kt in (12, 13, 14, 15):
                    emit_pv(P, kt)
                    ffn.emit(1)
                def fpc(eng):
                    eng.tensor_copy(pvstage[0:65, 0:512], PV0[0:65, :])
                    return eng.tensor_copy(pvstage[0:65, 512:1024], PV1[0:65, :])
                ev_pvcopy[P] = sch.add("v", fpc,
                                       waits=[ev_pv[(P, 15)], ev_mult.get(P - 1)],
                                       inc=True)
                ffn.emit(2)
            emit_recip(31)
            emit_norm(31)
            emit_mult(31)

            ev_wodma = {}
            ev_womm = {}
            ev_woev_i = {}
            ev_woev = {}
            ev_xt = {}
            ev_pp = {}
            ev_out = {}
            ev_silu = {}
            gt_off = 8192
            xt_off = 10240
            os_off = 14336

            def emit_xt(df):
                def fx(eng, df=df):
                    return eng.dma_start(
                        out=BE[:, xt_off + (df % 2) * 1024: xt_off + (df % 2) * 1024 + 1024],
                        in_=xfull[df * 128:(df + 1) * 128, 0:TOWN])
                ev_xt[df] = sch.add("g", fx, waits=[ev_pp.get(df - 2)],
                                    inc=True, inc_n=16)

            def emit_p6(df):
                def fs(eng, df=df):
                    return eng.activation(
                        BE[:, gt_off + (df % 2) * 1024: gt_off + (df % 2) * 1024 + 1024],
                        WS[:, df * 1024:(df + 1) * 1024], AF.Silu,
                        scale=dsos[:, df:df + 1])
                ev_silu[df] = sch.add("a", fs,
                                      waits=[ffn.ev_sacc[(15, df, 1)],
                                             ev_pp.get(df - 2)], inc=True)
                def fp6(eng, df=df):
                    g = BE[:, gt_off + (df % 2) * 1024: gt_off + (df % 2) * 1024 + 1024]
                    eng.tensor_tensor(g, g, A2[:, df * TOWN:(df + 1) * TOWN],
                                      op=ALU.mult)
                    eng.drain()
                    return eng.tensor_tensor(
                        BE[:, os_off + (df % 2) * 1024: os_off + (df % 2) * 1024 + 1024],
                        BE[:, xt_off + (df % 2) * 1024: xt_off + (df % 2) * 1024 + 1024],
                        g, op=ALU.add)
                ev_pp[df] = sch.add("v", fp6,
                                    waits=[ev_silu[df], ev_woev[(df, 1)],
                                           ev_xt[df], ev_out.get(df - 2)],
                                    inc=True)
                def fo(eng, df=df):
                    return eng.dma_start(
                        out=out[df * 128:(df + 1) * 128, :],
                        in_=BE[:, os_off + (df % 2) * 1024: os_off + (df % 2) * 1024 + 1024])
                ev_out[df] = sch.add("g", fo, waits=[ev_pp[df]],
                                     inc=True, inc_n=16)
                if df + 2 < 16:
                    emit_xt(df + 2)

            def emit_wodma(w_):
                sl, hf = w_ // 2, w_ % 2
                def fw(eng, w_=w_, sl=sl, hf=hf):
                    dst = BE[:, (w_ % 2) * 4096:(w_ % 2 + 1) * 4096]
                    return eng.dma_start(
                        out=dst.rearrange("p (c j) -> p c j", j=256),
                        in_=wo_r[:, :, sl * 512 + hf * 256: sl * 512 + hf * 256 + 256])
                ww = [ev_womm.get(w_ - 2)]
                if w_ == 0:
                    ww.append(ev_pv[(31, 11)])
                if w_ == 1:
                    ww.append(ev_pv[(31, 15)])
                ev_wodma[w_] = sch.add("g", fw, waits=ww, inc=True, inc_n=16)

            emit_wodma(0)
            emit_wodma(1)
            emit_xt(0)
            emit_xt(1)
            wo_i = 0
            for w_ in range(8):
                sl, hf = w_ // 2, w_ % 2
                if w_ >= 1:
                    emit_p6(2 * (w_ - 1))
                    emit_p6(2 * (w_ - 1) + 1)
                for fsub in range(2):
                    ef = sl * 4 + hf * 2 + fsub
                    for tc2 in range(2):
                        ps = psq[wo_i % 2]
                        def fm(eng, w_=w_, fsub=fsub, tc2=tc2, ps=ps):
                            base = (w_ % 2) * 4096
                            for c in range(16):
                                ins = eng.matmul(
                                    ps,
                                    BE[:, base + c * 256 + fsub * 128: base + c * 256 + fsub * 128 + 128],
                                    A1[:, c * TOWN + tc2 * 512: c * TOWN + tc2 * 512 + 512],
                                    start=(c == 0), stop=(c == 15))
                            return ins
                        ww2 = [ev_wodma[w_], ev_woev_i.get(wo_i - 2)]
                        if wo_i == 0:
                            ww2 += [ev_mult[31], ev_aodma[31],
                                    ev_exp[(31, 15)]] + sch.dma_tokens("s")
                        if wo_i == 1:
                            ww2.append(ev_exp[(31, 15)])
                        ev_womm[w_] = sch.add("t", fm, waits=ww2, inc=True)
                        def fe(eng, ef=ef, tc2=tc2, ps=ps):
                            return eng.activation(
                                A2[:, ef * TOWN + tc2 * 512: ef * TOWN + tc2 * 512 + 512],
                                ps, AF.Identity, bias=bo_sb[:, ef:ef + 1], scale=1.0)
                        ev_woev_i[wo_i] = ev_woev[(ef, tc2)] = \
                            sch.add("a", fe, waits=[ev_womm[w_]], inc=True)
                        wo_i += 1
                ffn.emit(2)
                if w_ + 2 < 8:
                    emit_wodma(w_ + 2)
            while ffn.emit(1):
                pass
            emit_p6(14)
            emit_p6(15)
            sch.add("g", lambda eng: eng.nop(), waits=sch.dma_tokens("g"))
            sch.add("s", lambda eng: eng.nop(), waits=sch.dma_tokens("s"))
            sch.run(blk)

    return nc



def _head_perm():
    perm = np.zeros(D, dtype=np.int64)
    for c in range(4):
        for i in range(4):
            for b in range(2):
                h = 8 * c + i + 4 * b
                newbase = (4 * c + i) * 128 + b * 64
                perm[newbase:newbase + 64] = h * 64 + np.arange(64)
    return perm


def _prep_inputs(L0_attn_x, L0_attn_Wqkv, L0_attn_bqkv, L0_attn_Wo, L0_attn_bo,
                 L0_ffn_up_weight, L0_ffn_down_weight, L0_ffn_up_scale,
                 L0_ffn_down_scale, L0_ffn_output_scale):
    perm = _head_perm()
    wq = np.asarray(L0_attn_Wqkv)[:, :D][:, perm]
    wkv = np.asarray(L0_attn_Wqkv)[:, D:]
    wqkv = np.ascontiguousarray(np.concatenate([wq, wkv], axis=1), dtype=np.float32)
    bq = np.asarray(L0_attn_bqkv)[:D][perm]
    bqkv = np.ascontiguousarray(np.concatenate([bq, np.asarray(L0_attn_bqkv)[D:]]),
                                dtype=np.float32)
    wo = np.ascontiguousarray(np.asarray(L0_attn_Wo)[perm, :], dtype=np.float32)
    upt = np.ascontiguousarray(np.asarray(L0_ffn_up_weight).T, dtype=np.float32)
    dnt = np.ascontiguousarray(np.asarray(L0_ffn_down_weight).T, dtype=np.float32)
    xT = np.ascontiguousarray(np.asarray(L0_attn_x).transpose(0, 2, 1), dtype=np.float32)

    common = {
        "wqkv": wqkv, "bqkv": bqkv, "wo": wo,
        "bo": np.ascontiguousarray(np.asarray(L0_attn_bo), dtype=np.float32),
        "upt": upt, "dnt": dnt,
        "us": np.ascontiguousarray(np.asarray(L0_ffn_up_scale), dtype=np.float32),
        "ds": np.ascontiguousarray(np.asarray(L0_ffn_down_scale), dtype=np.float32),
        "osc": np.ascontiguousarray(np.asarray(L0_ffn_output_scale), dtype=np.float32),
    }
    in_maps = []
    for core in range(8):
        b, half = core // 2, core % 2
        xb = xT[b]
        if half == 0:
            xc = xb
        else:
            xc = np.concatenate([xb[:, TOWN:], xb[:, :TOWN]], axis=1)
        m = dict(common)
        m["x"] = np.ascontiguousarray(xc)
        in_maps.append(m)
    return in_maps


def kernel(**inputs):
    in_maps = _prep_inputs(**inputs)
    nc = build_program()
    res = run_bass_kernel_spmd(nc, in_maps, core_ids=list(range(8)))
    outT = np.zeros((4, D, S), dtype=np.float32)
    for core in range(8):
        b, half = core // 2, core % 2
        outT[b][:, half * TOWN:(half + 1) * TOWN] = res.results[core]["out"]
    return np.ascontiguousarray(outT.transpose(0, 2, 1))


# version 16
# speedup vs baseline: 1.1975x; 1.1975x over previous
import numpy as np
import ml_dtypes
import concourse.bass as bass
import concourse.mybir as mybir
from concourse.bass_utils import run_bass_kernel_spmd

F32 = mybir.dt.float32
BF16 = mybir.dt.bfloat16
AF = mybir.ActivationFunctionType
ALU = mybir.AluOpType

D = 2048
S = 2048
TOWN = 1024
NQ, NKV, HD = 32, 8, 64
H = 8192
SHIFT = -20.0

ENGINES = ("t", "a", "v", "g", "s")
SEMS = ("t", "a", "v", "g0", "g1", "g2", "g3", "s0", "s1", "s2", "s3")


class Sched:

    def __init__(self, nc):
        self.nc = nc
        self.sems = {}
        self.cnt = {s: 0 for s in SEMS}
        self.maxw = {}
        self.dma_idx = {"g": 0, "s": 0}
        self.last_dma_tok = {"g": {}, "s": {}}

    def open(self):
        for s in SEMS:
            self.sems[s] = self.nc.alloc_semaphore(f"sem_{s}")

    def begin_phase(self):
        self.ops = {e: [] for e in ENGINES}

    def add(self, e, fn, waits=(), inc=False, inc_n=1):
        if e in ("g", "s") and inc:
            semkey = e + str(self.dma_idx[e] % 4)
            self.dma_idx[e] += 1
        else:
            semkey = e
        w = []
        for t in waits:
            if not t:
                continue
            sk, v = t
            if v <= 0:
                continue
            if sk == e and e in ("t", "a", "v"):
                continue
            w.append((sk, v))
        self.ops[e].append((w, fn, inc, inc_n, semkey))
        if inc:
            self.cnt[semkey] += inc_n
            if e in ("g", "s"):
                self.last_dma_tok[e][semkey] = (semkey, self.cnt[semkey])
        return (semkey, self.cnt.get(semkey, 0))

    def dma_tokens(self, e):
        return list(self.last_dma_tok[e].values())

    def run(self, block):
        def runner(e):
            def go(eng):
                prev_was_dma = [False]
                for waits, fn, inc, inc_n, semkey in self.ops[e]:
                    emitted = False
                    for (sk, v) in waits:
                        key = (e, sk)
                        if self.maxw.get(key, 0) < v:
                            eng.wait_ge(self.sems[sk], v)
                            self.maxw[key] = v
                            emitted = True
                    is_dma = inc and e in ("g", "s")
                    if is_dma and prev_was_dma[0] and not emitted:
                        eng.wait_ge(self.sems[semkey],
                                    self.maxw.get((e, semkey), 0))
                    ins = fn(eng)
                    if inc:
                        ins.then_inc(self.sems[semkey], inc_n)
                    prev_was_dma[0] = is_dma
            return go

        block.tensor(runner("t"))
        block.scalar(runner("a"))
        block.vector(runner("v"))
        block.gpsimd(runner("g"))
        block.sync(runner("s"))


def build_program():
    nc = bass.Bass("TRN2", target_bir_lowering=False)

    xfull = nc.dram_tensor("x", [D, S], F32, kind="ExternalInput").ap()
    wqkv = nc.dram_tensor("wqkv", [D, 3072], F32, kind="ExternalInput").ap()
    bqkv = nc.dram_tensor("bqkv", [3072], F32, kind="ExternalInput").ap()
    wo = nc.dram_tensor("wo", [D, D], F32, kind="ExternalInput").ap()
    bo = nc.dram_tensor("bo", [D], F32, kind="ExternalInput").ap()
    upt = nc.dram_tensor("upt", [D, H], F32, kind="ExternalInput").ap()
    dnt = nc.dram_tensor("dnt", [H, D], F32, kind="ExternalInput").ap()
    us = nc.dram_tensor("us", [H], F32, kind="ExternalInput").ap()
    ds = nc.dram_tensor("ds", [D], F32, kind="ExternalInput").ap()
    osc = nc.dram_tensor("osc", [1], F32, kind="ExternalInput").ap()
    out = nc.dram_tensor("out", [D, TOWN], F32, kind="ExternalOutput").ap()

    xfull_r = xfull.rearrange("(c p) t -> p c t", p=128)
    wqkv_r = wqkv.rearrange("(c p) f -> p c f", p=128)
    wo_r = wo.rearrange("(c p) e -> p c e", p=128)
    upt_r = upt.rearrange("(c p) h -> p c h", p=128)
    dnt_r = dnt.rearrange("(c p) f -> p c f", p=128)

    sch = Sched(nc)
    sch.open()

    from contextlib import ExitStack
    with ExitStack() as _ctx:
        _e = _ctx.enter_context
        ones_bf = _e(nc.sbuf_tensor("ones_bf", [128, 1], BF16))
        onesf = _e(nc.sbuf_tensor("onesf", [128, 128], F32))
        onesb2 = _e(nc.sbuf_tensor("onesb2", [128, 128], BF16))
        c0125 = _e(nc.sbuf_tensor("c0125", [128, 1], F32))
        cn20 = _e(nc.sbuf_tensor("cn20", [128, 1], F32))
        ceps = _e(nc.sbuf_tensor("ceps", [128, 1], F32))
        cinvn = _e(nc.sbuf_tensor("cinvn", [128, 1], F32))
        bq_sb = _e(nc.sbuf_tensor("bq_sb", [128, 24], F32))
        qbias = _e(nc.sbuf_tensor("qbias", [128, 16], F32))
        bo_sb = _e(nc.sbuf_tensor("bo_sb", [128, 16], F32))
        us_sb = _e(nc.sbuf_tensor("us_sb", [128, 64], F32))
        ds_sb = _e(nc.sbuf_tensor("ds_sb", [128, 16], F32))
        dsos = _e(nc.sbuf_tensor("dsos", [128, 16], F32))
        os_sb = _e(nc.sbuf_tensor("os_sb", [1, 1], F32))
        os_col = _e(nc.sbuf_tensor("os_col", [128, 1], F32))
        vb_bc = _e(nc.sbuf_tensor("vb_bc", [128, 512], BF16))
        A1 = _e(nc.sbuf_tensor("A1", [128, 16 * TOWN], BF16))
        BE = _e(nc.sbuf_tensor("BE", [128, 16 * TOWN], BF16))
        Cxn = _e(nc.sbuf_tensor("Cxn", [128, 16 * TOWN], BF16))
        A2 = _e(nc.sbuf_tensor("A2", [128, 4 * S + 16 * 520], BF16))
        WS = _e(nc.sbuf_tensor("WS", [128, 16384], BF16))
        Hb = _e(nc.sbuf_tensor("Hb", [128, 4096], BF16))

        Ksb = A2[:, 0:4 * S]
        Vaug_off = 4 * S
        W1 = WS[:, 0:8192]
        W2 = WS[:, 8192:16384]
        ering = BE[:, 0:8192]

        ev_xl = {}
        pre_toks = []
        sch.begin_phase()
        with nc.sbuf_tensor("vb_row", [1, 512], BF16) as vb_row, \
             nc.psum_tensor("p0a", [128, 1], F32) as p0a, \
             nc.psum_tensor("p0b", [128, 512], F32) as p0b, \
             nc.Block("blk_p0") as blk:
            def f(eng):
                eng.memset(ones_bf[:, :], 1.0)
                eng.memset(onesf[:, :], 1.0)
                eng.memset(onesb2[:, :], 1.0)
                eng.memset(c0125[:, :], 0.125)
                eng.memset(cn20[:, :], SHIFT)
                eng.memset(ceps[:, :], 1e-5)
                return eng.memset(cinvn[:, :], 1.0 / D)
            ev_ms = sch.add("v", f, inc=True)

            def f(eng):
                return eng.dma_start(out=os_sb[:, :],
                                     in_=osc.rearrange("(a b) -> a b", a=1),
                                     allow_slow_non_contiguous=True)
            sch.add("s", f, inc=True, inc_n=16)
            def f(eng):
                return eng.dma_start(out=vb_row[:, :],
                                     in_=bqkv[2560:3072].rearrange("(a b) -> a b", a=1),
                                     allow_slow_non_contiguous=True)
            ev_vbl = sch.add("g", f, inc=True, inc_n=16)
            for pc in range(2):
                def f(eng, pc=pc):
                    return eng.dma_start(
                        out=A1.ap().rearrange("p (c t) -> p c t", t=TOWN)[:, pc * 8:(pc + 1) * 8, :],
                        in_=xfull_r[:, pc * 8:(pc + 1) * 8, 0:TOWN])
                ev_xl[(0, pc)] = sch.add("g", f, inc=True, inc_n=16)
            def f(eng):
                return eng.dma_start(
                    out=W1.rearrange("p (c j) -> p c j", j=512),
                    in_=wqkv_r[:, :, 0:512])
            pre_toks.append(sch.add("g", f, inc=True, inc_n=16))
            for pc in range(2):
                def f(eng, pc=pc):
                    return eng.dma_start(
                        out=BE.ap().rearrange("p (c t) -> p c t", t=TOWN)[:, pc * 8:(pc + 1) * 8, :],
                        in_=xfull_r[:, pc * 8:(pc + 1) * 8, TOWN:S])
                ev_xl[(1, pc)] = sch.add("g", f, inc=True, inc_n=16)
            def f(eng):
                return eng.dma_start(
                    out=W2.rearrange("p (c j) -> p c j", j=512),
                    in_=wqkv_r[:, :, 512:1024])
            pre_toks.append(sch.add("g", f, inc=True, inc_n=16))

            def f(eng):
                eng.matmul(p0a[:, :], onesf[0:1, :], os_sb[0:1, 0:1], start=True, stop=True)
                return eng.matmul(p0b[:, :], onesb2[0:1, :], vb_row[0:1, :], start=True, stop=True)
            ev_bc = sch.add("t", f, waits=[ev_ms, ev_vbl] + sch.dma_tokens("s"), inc=True)

            def f(eng):
                eng.tensor_copy(os_col[:, :], p0a[:, :])
                return eng.tensor_copy(vb_bc[:, :], p0b[:, :])
            ev_cp = sch.add("v", f, waits=[ev_bc], inc=True)
            sch.run(blk)

        ev_aff = {}
        ev_bcv_h1 = None
        ev_sqmm = {}
        ev_stats_prev = [None]
        with nc.sbuf_tensor("rows_f", [1, 3072], F32) as rows_f, \
             nc.sbuf_tensor("rows_b", [1, 2048], BF16) as rows_b, \
             nc.sbuf_tensor("Ff", [128, 2048], BF16) as Ff:
            r_mean = rows_f[0:1, 0:1024]
            r_var = rows_f[0:1, 1024:2048]
            r_tmp = rows_f[0:1, 2048:3072]
            r_rstd = rows_b[0:1, 0:1024]
            r_b = rows_b[0:1, 1024:2048]

            def emit_ln_stats(half, xsrc, ps_s, ps_q):
                hb_off = half * 2048
                for fc in range(16):
                    gfc = half * 16 + fc
                    def f(eng, fc=fc, xsrc=xsrc, ps_s=ps_s):
                        for n in range(2):
                            ins = eng.matmul(
                                ps_s[0:1, n * 512:(n + 1) * 512], ones_bf[:, :],
                                xsrc[:, fc * TOWN + n * 512: fc * TOWN + (n + 1) * 512],
                                start=(fc == 0), stop=(fc == 15))
                        return ins
                    sch.add("t", f, waits=[ev_xl[(half, fc // 8)], ev_stats_prev[0]], inc=True)
                    def f(eng, fc=fc, xsrc=xsrc, hb_off=hb_off):
                        return eng.activation(
                            Hb[:, hb_off + (fc % 2) * 1024: hb_off + (fc % 2) * 1024 + 1024],
                            xsrc[:, fc * TOWN:(fc + 1) * TOWN], AF.Square)
                    ev_sq = sch.add("a", f,
                                    waits=[ev_xl[(half, fc // 8)],
                                           ev_sqmm.get(gfc - 2)],
                                    inc=True)
                    def f(eng, fc=fc, hb_off=hb_off, ps_q=ps_q):
                        for n in range(2):
                            ins = eng.matmul(
                                ps_q[0:1, n * 512:(n + 1) * 512], ones_bf[:, :],
                                Hb[:, hb_off + (fc % 2) * 1024 + n * 512:
                                   hb_off + (fc % 2) * 1024 + (n + 1) * 512],
                                start=(fc == 0), stop=(fc == 15))
                        return ins
                    ev_sqmm[gfc] = sch.add("t", f,
                                           waits=[ev_sq, ev_stats_prev[0]],
                                           inc=True)
                def f(eng, ps_s=ps_s, ps_q=ps_q):
                    eng.tensor_scalar(r_mean, ps_s[0:1, :], cinvn[0:1, 0:1], None, op0=ALU.mult)
                    eng.tensor_scalar(r_var, ps_q[0:1, :], cinvn[0:1, 0:1], None, op0=ALU.mult)
                    eng.drain()
                    eng.tensor_tensor(r_tmp, r_mean, r_mean, op=ALU.mult)
                    eng.drain()
                    return eng.tensor_tensor(r_var, r_var, r_tmp, op=ALU.subtract)
                ev_st = sch.add("v", f, waits=[ev_sqmm[half * 16 + 15]], inc=True)
                ev_stats_prev[0] = ev_st
                def f(eng):
                    eng.activation(r_tmp, r_var, AF.Ln, bias=ceps[0:1, 0:1])
                    eng.drain()
                    return eng.activation(r_rstd, r_tmp, AF.Exp, scale=-0.5)
                ev_rs = sch.add("a", f, waits=[ev_st], inc=True)
                def f(eng):
                    return eng.scalar_tensor_tensor(
                        r_b, r_mean, -1.0, r_rstd, op0=ALU.mult, op1=ALU.mult)
                return sch.add("v", f, waits=[ev_rs], inc=True)

            def emit_ln_affine(half, ev_bb, xsrc, xn_dst, ps_b, ps_c):
                ev_bcv = None
                for n in range(2):
                    def f(eng, n=n, ps_b=ps_b, ps_c=ps_c):
                        eng.matmul(ps_b[:, :], onesb2[0:1, :],
                                   r_rstd[0:1, n * 512:(n + 1) * 512],
                                   start=True, stop=True)
                        return eng.matmul(ps_c[:, :], onesb2[0:1, :],
                                          r_b[0:1, n * 512:(n + 1) * 512],
                                          start=True, stop=True)
                    ev_bct0 = sch.add("t", f, waits=[ev_bb, ev_bcv], inc=True)
                    def f(eng, n=n, ps_b=ps_b, ps_c=ps_c):
                        eng.tensor_copy(Ff[:, n * 512:(n + 1) * 512], ps_b[:, :])
                        return eng.tensor_copy(Ff[:, 1024 + n * 512:1024 + (n + 1) * 512],
                                               ps_c[:, :])
                    ev_bcv = sch.add("v", f, waits=[ev_bct0], inc=True)
                for fc in range(16):
                    def f(eng, fc=fc, xsrc=xsrc, xn_dst=xn_dst):
                        if fc == 0:
                            eng.drain()
                        d = xn_dst[:, fc * TOWN:(fc + 1) * TOWN]
                        return eng.tensor_tensor(d, xsrc[:, fc * TOWN:(fc + 1) * TOWN],
                                                 Ff[:, 0:1024], op=ALU.mult)
                    sch.add("v", f, waits=[ev_bcv], inc=True)
                for fc in range(16):
                    gfc = half * 16 + fc
                    def f(eng, fc=fc, xn_dst=xn_dst):
                        d = xn_dst[:, fc * TOWN:(fc + 1) * TOWN]
                        return eng.tensor_tensor(d, d, Ff[:, 1024:2048], op=ALU.add)
                    ev_aff[gfc] = sch.add("v", f, inc=True)
                return ev_bcv

            sch.begin_phase()
            with nc.psum_tensor("p1s", [1, 1024], F32) as p1s, \
                 nc.psum_tensor("p1q", [1, 1024], F32) as p1q, \
                 nc.psum_tensor("p1b", [128, 512], F32) as p1b, \
                 nc.psum_tensor("p1c", [128, 512], F32) as p1c, \
                 nc.Block("blk_p1") as blk:
                ev_vm = sch.add("v", lambda eng: eng.memset(A2[:, Vaug_off:], 1.0), inc=True)
                gath = {}
                for e, key, dst, src, c0, c1 in (
                    ("s", "bq", bq_sb, bqkv, 0, 24),
                    ("s", "us0", us_sb, us, 0, 32),
                    ("s", "bo", bo_sb, bo, 0, 16),
                    ("s", "ds", ds_sb, ds, 0, 16),
                ):
                    def f(eng, dst=dst, src=src, c0=c0, c1=c1):
                        return eng.dma_start(
                            out=dst[:, c0:c1],
                            in_=src[c0 * 128:c1 * 128].rearrange("(a b) -> b a", b=128),
                            allow_slow_non_contiguous=True)
                    gath[key] = sch.add(e, f, inc=True, inc_n=16)
                ev_bb0 = emit_ln_stats(0, A1, p1s, p1q)
                def f(eng):
                    return eng.dma_start(
                        out=us_sb[:, 32:64],
                        in_=us[32 * 128:64 * 128].rearrange("(a b) -> b a", b=128),
                        allow_slow_non_contiguous=True)
                gath["us1"] = sch.add("a", f, inc=True, inc_n=16)
                def f(eng):
                    eng.mul(qbias[:, :], bq_sb[:, 0:16], c0125[:, 0:1])
                    return eng.mul(dsos[:, :], ds_sb[:, :], os_col[:, 0:1])
                sch.add("a", f, waits=[gath["bq"], gath["ds"], ev_cp], inc=True)
                emit_ln_affine(0, ev_bb0, A1, Cxn, p1b, p1c)
                sch.run(blk)

        sch.begin_phase()
        with nc.sbuf_tensor("Wa", [128, 8192], BF16) as Wa, \
             nc.sbuf_tensor("Wb", [128, 8192], BF16) as Wb, \
             nc.sbuf_tensor("pvstage", [128, 1024], F32) as pvstage, \
             nc.psum_tensor("R0", [128, 1024], F32) as R0, \
             nc.psum_tensor("R1", [128, 1024], F32) as R1, \
             nc.psum_tensor("PVF", [128, 2048], F32) as PVF, \
             nc.Block("blk_mega") as blk:
            PV0 = PVF[:, 0:512]
            PV1 = PVF[:, 512:1024]
            FF = (PVF[:, 1024:1536], PVF[:, 1536:2048])
            regs = (R0, R1)
            ao_tmp = BE[0:64, 12288:13312]
            bc_sb = BE[:, 13312:13824]
            bc_sb2 = BE[:, 13824:14336]
            recb = BE[:, 13312:14336]

            class FStream:
                def __init__(self):
                    self.seq = []
                    for hb in range(16):
                        for hc in range(4):
                            for tcx in range(2):
                                self.seq.append(("u", hb, hc, tcx))
                        for df in range(16):
                            for tcx in range(2):
                                self.seq.append(("d", hb, df, tcx))
                    self.pos = 0
                    self.n = 0
                    self.ev_evac = {}
                    self.ev_upg = {}
                    self.ev_dng = {}
                    self.ev_signu = {}
                    self.ev_signd = {}
                    self.ev_relu = {}
                    self.ev_sacc = {}
                    self.slab_u = set()
                    self.slab_d = set()
                    self.ddma = {}
                    self.udma = {}

                def _emit_uraw(self, hb):
                    for half in (0, 1):
                        def f(eng, hb=hb, half=half):
                            return eng.dma_start(
                                out=Wa.ap().rearrange("p (c j) -> p c j", j=512)[:, half * 8:(half + 1) * 8, :],
                                in_=upt_r[:, half * 8:(half + 1) * 8, hb * 512:(hb + 1) * 512])
                        w = [self.ev_upg.get(hb - 1)]
                        if hb == 0:
                            w += [ev_aff[31], ev_bcv_h1]
                        self.udma[(hb, half)] = sch.add("g", f, waits=w,
                                                        inc=True, inc_n=16)

                def _emit_signu_chunk(self, hb, q):
                    def fs(eng, q=q):
                        reg = Wa[:, q * 2048:(q + 1) * 2048]
                        return eng.sign(reg, reg)
                    self.ev_signu[hb] = sch.add(
                        "a", fs, waits=[self.udma[(hb, q // 2)]], inc=True)

                def _emit_slab_u(self, hb):
                    self.slab_u.add(hb)
                    self._emit_uraw(hb)
                    for q in range(4):
                        self._emit_signu_chunk(hb, q)

                def _emit_slab_d(self, hb):
                    self.slab_d.add(hb)
                    for half in (0, 1):
                        def f(eng, hb=hb, half=half):
                            return eng.dma_start(
                                out=Wb.ap().rearrange("p (c j) -> p c j", j=2048)[:, half * 2:(half + 1) * 2, :],
                                in_=dnt_r[:, hb * 4 + half * 2: hb * 4 + half * 2 + 2, :])
                        w = [self.ev_dng.get(hb - 1)]
                        if hb == 0:
                            w += [ev_aff[31], ev_bcv_h1]
                        self.ddma[(hb, half)] = sch.add("g", f, waits=w,
                                                        inc=True, inc_n=16)

                def _emit_signd_chunk(self, hb, q):
                    def fs(eng, q=q):
                        reg = Wb[:, q * 2048:(q + 1) * 2048]
                        return eng.sign(reg, reg)
                    self.ev_signd[hb] = sch.add(
                        "a", fs, waits=[self.ddma[(hb, q // 2)]], inc=True)

                def emit(self, k, pred=None):
                    c = 0
                    while c < k and self.pos < len(self.seq):
                        it = self.seq[self.pos]
                        if pred is not None and not pred(it):
                            break
                        kind, hb, x, tcx = it
                        n = self.n
                        ff = FF[n % 2]
                        if kind == "u":
                            if hb not in self.slab_u:
                                self._emit_slab_u(hb)
                            if hb not in self.slab_d:
                                self._emit_slab_d(hb)
                            w = [self.ev_signu[hb], self.ev_evac.get(n - 2)]
                            if hb == 0:
                                w.append(ev_aff[15])
                            def f(eng, hc=x, tcx=tcx, ff=ff):
                                for c_ in range(16):
                                    ins = eng.matmul(
                                        ff[:, :],
                                        Wa[:, c_ * 512 + hc * 128: c_ * 512 + hc * 128 + 128],
                                        Cxn[:, c_ * TOWN + tcx * 512: c_ * TOWN + tcx * 512 + 512],
                                        start=(c_ == 0), stop=(c_ == 15))
                                return ins
                            tok = sch.add("t", f, waits=w, inc=True)
                            self.ev_upg[hb] = tok
                            def fr(eng, hb=hb, hc=x, tcx=tcx, ff=ff):
                                return eng.activation(
                                    Hb[:, hc * 1024 + tcx * 512: hc * 1024 + tcx * 512 + 512],
                                    ff[:, :], AF.Relu,
                                    scale=us_sb[:, hb * 4 + hc: hb * 4 + hc + 1])
                            rt = sch.add("a", fr, waits=[tok, self.ev_dng.get(hb - 1)], inc=True)
                            self.ev_relu[(hb, tcx)] = rt
                            self.ev_evac[n] = rt
                            gidx = x * 2 + tcx
                            if 3 <= gidx <= 6:
                                self._emit_signd_chunk(hb, gidx - 3)
                        else:
                            didx = x * 2 + tcx
                            if didx == 0 and hb + 1 < 16 and hb + 1 not in self.slab_u:
                                self.slab_u.add(hb + 1)
                                self._emit_uraw(hb + 1)
                            if didx in (24, 26, 28, 30) and hb + 1 < 16:
                                self._emit_signu_chunk(hb + 1, (didx - 24) // 2)
                            w = [self.ev_signd[hb], self.ev_relu[(hb, tcx)],
                                 self.ev_evac.get(n - 2)]
                            def f(eng, df=x, tcx=tcx, ff=ff):
                                for hc in range(4):
                                    ins = eng.matmul(
                                        ff[:, :],
                                        Wb[:, hc * 2048 + df * 128: hc * 2048 + df * 128 + 128],
                                        Hb[:, hc * 1024 + tcx * 512: hc * 1024 + tcx * 512 + 512],
                                        start=(hc == 0), stop=(hc == 3))
                                return ins
                            tok = sch.add("t", f, waits=w, inc=True)
                            self.ev_dng[hb] = tok
                            def fv(eng, hb=hb, df=x, tcx=tcx, ff=ff):
                                dst = WS[:, df * 1024 + tcx * 512: df * 1024 + tcx * 512 + 512]
                                if hb == 0:
                                    return eng.tensor_copy(dst, ff[:, :])
                                return eng.tensor_tensor(dst, dst, ff[:, :], op=ALU.add)
                            vt = sch.add("v", fv, waits=[tok], inc=True)
                            self.ev_evac[n] = vt
                            self.ev_sacc[(hb, x, tcx)] = vt
                        self.n += 1
                        self.pos += 1
                        c += 1
                    return c

            ffn = FStream()
            ev_bb1 = None

            psq = [R0[:, 0:512], R0[:, 512:1024]]
            psv = [R1[:, 0:512], R1[:, 512:1024]]
            ev_mm_sl = {}
            ev_mm_first = {}
            ev_ev = {}
            ev_vevac = {}
            ev_dma_sl = {}
            ev_kevac_last = None
            qv_i = 0
            kv_i = 0
            for sl in range(6):
                if sl == 1:
                    ev_bb1 = emit_ln_stats(1, BE, PVF[0:1, 0:1024],
                                           PVF[0:1, 1024:2048])
                if sl == 2:
                    ev_bcv_h1 = emit_ln_affine(1, ev_bb1, BE, BE,
                                               PVF[:, 0:512], PVF[:, 512:1024])
                Wb_ = W1 if sl % 2 == 0 else W2
                if sl < 2:
                    ev_dma_sl[sl] = pre_toks[sl]
                else:
                    def f(eng, sl=sl, Wb_=Wb_):
                        return eng.dma_start(
                            out=Wb_.rearrange("p (c j) -> p c j", j=512),
                            in_=wqkv_r[:, :, sl * 512:(sl + 1) * 512])
                    ev_dma_sl[sl] = sch.add("g", f,
                                            waits=[ev_mm_sl.get(sl - 2),
                                                   ev_mm_first.get(sl - 1)],
                                            inc=True, inc_n=16)
                if sl < 4:
                    for fsub in range(4):
                        qf = sl * 4 + fsub
                        for tc in range(2):
                            ps = psq[qv_i % 2]
                            def f(eng, Wb_=Wb_, fsub=fsub, tc=tc, ps=ps):
                                for c in range(16):
                                    ins = eng.matmul(
                                        ps,
                                        Wb_[:, c * 512 + fsub * 128: c * 512 + fsub * 128 + 128],
                                        Cxn[:, c * TOWN + tc * 512: c * TOWN + tc * 512 + 512],
                                        start=(c == 0), stop=(c == 15))
                                return ins
                            ev_mm = sch.add(
                                "t", f,
                                waits=[ev_dma_sl[sl], ev_aff[15],
                                       ev_ev.get(("q", qv_i % 2))],
                                inc=True)
                            ev_mm_first.setdefault(sl, ev_mm)
                            ev_mm_sl[sl] = ev_mm
                            def f(eng, qf=qf, tc=tc, ps=ps):
                                return eng.activation(
                                    A1[:, qf * TOWN + tc * 512: qf * TOWN + tc * 512 + 512],
                                    ps, AF.Identity,
                                    bias=qbias[:, qf:qf + 1], scale=0.125)
                            ev_ev[("q", qv_i % 2)] = sch.add("a", f, waits=[ev_mm], inc=True)
                            qv_i += 1
                elif sl == 4:
                    for fsub in range(4):
                        for tcf in range(4):
                            ps = psq[qv_i % 2]
                            src = Cxn if tcf < 2 else BE
                            tcl = tcf % 2
                            hh = 0 if tcf < 2 else 1
                            def f(eng, Wb_=Wb_, fsub=fsub, src=src, tcl=tcl, ps=ps):
                                for c in range(16):
                                    ins = eng.matmul(
                                        ps,
                                        Wb_[:, c * 512 + fsub * 128: c * 512 + fsub * 128 + 128],
                                        src[:, c * TOWN + tcl * 512: c * TOWN + tcl * 512 + 512],
                                        start=(c == 0), stop=(c == 15))
                                return ins
                            ev_mm = sch.add(
                                "t", f,
                                waits=[ev_dma_sl[sl], ev_aff[hh * 16 + 15],
                                       ev_ev.get(("q", qv_i % 2))],
                                inc=True)
                            ev_mm_first.setdefault(sl, ev_mm)
                            ev_mm_sl[sl] = ev_mm
                            def f(eng, fsub=fsub, tcf=tcf, ps=ps):
                                return eng.activation(
                                    Ksb[:, fsub * S + tcf * 512: fsub * S + tcf * 512 + 512],
                                    ps, AF.Identity,
                                    bias=bq_sb[:, 16 + fsub:17 + fsub], scale=1.0)
                            ev_kevac_last = ev_ev[("q", qv_i % 2)] = \
                                sch.add("a", f, waits=[ev_mm], inc=True)
                            qv_i += 1
                else:
                    for tt in range(16):
                        ps = psv[kv_i % 2]
                        src = Cxn if tt < 8 else BE
                        ttl = tt % 8
                        hh = 0 if tt < 8 else 1
                        def f(eng, Wb_=Wb_, src=src, ttl=ttl, ps=ps):
                            for c in range(16):
                                ins = eng.matmul(
                                    ps,
                                    src[:, c * TOWN + ttl * 128: c * TOWN + ttl * 128 + 128],
                                    Wb_[:, c * 512: c * 512 + 512],
                                    start=(c == 0), stop=(c == 15))
                            return ins
                        ev_mm = sch.add(
                            "t", f,
                            waits=[ev_dma_sl[sl], ev_aff[hh * 16 + 15],
                                   ev_ev.get(("v", kv_i % 2)), ev_vm, ev_bcv_h1],
                            inc=True)
                        ev_mm_first.setdefault(sl, ev_mm)
                        ev_mm_sl[sl] = ev_mm
                        def f(eng, tt=tt, ps=ps):
                            o3 = A2[:, Vaug_off + tt * 520: Vaug_off + tt * 520 + 520]
                            return eng.tensor_tensor(
                                o3.rearrange("p (g e) -> p g e", e=65)[:, :, 0:64],
                                ps.rearrange("p (g e) -> p g e", e=64),
                                vb_bc[:, :].rearrange("p (g e) -> p g e", e=64),
                                op=ALU.add)
                        ev_vevac[tt] = ev_ev[("v", kv_i % 2)] = \
                            sch.add("v", f, waits=[ev_mm], inc=True)
                        kv_i += 1
                        if tt % 2 == 1:
                            ffn.emit(1, pred=lambda it: it[0] == "u" and it[1] == 0)

            ev_qk = {}
            ev_exp = {}
            ev_pv = {}
            ev_pvcopy = {}
            ev_recip = {}
            ev_bct = {}
            ev_bccopy = {}
            ev_mult = {}
            ev_aodma = {}

            def unit_params(P):
                c_ = P // 8
                i_ = (P % 8) // 2
                tc = P % 2
                return c_, i_, tc, 4 * c_ + i_, 2 * c_, 2 * c_ + 1

            def emit_recip(P):
                def frc(eng):
                    eng.activation(pvstage[64:65, 0:1024],
                                   pvstage[64:65, 0:1024], AF.Ln)
                    eng.drain()
                    return eng.activation(recb[64:65, 0:1024],
                                          pvstage[64:65, 0:1024], AF.Exp, scale=-1.0)
                ev_recip[P] = sch.add("a", frc,
                                      waits=[ev_pvcopy[P], ev_bct.get(P - 1)],
                                      inc=True)

            def emit_norm(P):
                def fb(eng):
                    eng.matmul(PV0[0:64, :], onesb2[64:65, 0:64],
                               recb[64:65, 0:512],
                               start=True, stop=True, tile_position=(64, 0))
                    return eng.matmul(PV1[0:64, :], onesb2[64:65, 0:64],
                                      recb[64:65, 512:1024],
                                      start=True, stop=True, tile_position=(64, 0))
                ev_bct[P] = sch.add("t", fb, waits=[ev_recip[P]], inc=True)
                def fc(eng):
                    eng.tensor_copy(bc_sb[0:64, :], PV0[0:64, :])
                    return eng.tensor_copy(bc_sb2[0:64, :], PV1[0:64, :])
                ev_bccopy[P] = sch.add("v", fc, waits=[ev_bct[P]], inc=True)

            def emit_mult(P):
                c_, i_, tc, qch, gA, gB = unit_params(P)
                slot = qch * TOWN + tc * 512
                def fm(eng, P=P, slot=slot):
                    eng.tensor_tensor(A1[0:64, slot:slot + 512],
                                      pvstage[0:64, 0:512], bc_sb[0:64, :],
                                      op=ALU.mult)
                    return eng.tensor_tensor(
                        ao_tmp[0:64, (P % 2) * 512:(P % 2) * 512 + 512],
                        pvstage[0:64, 512:1024], bc_sb2[0:64, :], op=ALU.mult)
                ev_mult[P] = sch.add("v", fm,
                                     waits=[ev_bccopy[P], ev_aodma.get(P - 2)],
                                     inc=True)
                def fd(eng, P=P, slot=slot):
                    return eng.dma_start(
                        out=A1[64:128, slot:slot + 512],
                        in_=ao_tmp[0:64, (P % 2) * 512:(P % 2) * 512 + 512])
                ev_aodma[P] = sch.add("s", fd, waits=[ev_mult[P]],
                                      inc=True, inc_n=16)

            def emit_pv(P, kt):
                c_, i_, tc, qch, gA, gB = unit_params(P)
                slot = (kt % 8) * 1024
                vb0 = Vaug_off + kt * 520 + gA * 65
                vb1 = Vaug_off + kt * 520 + gB * 65
                def fp(eng, kt=kt, slot=slot, vb0=vb0, vb1=vb1):
                    eng.matmul(PV0[0:65, :], A2[:, vb0:vb0 + 65],
                               ering[:, slot:slot + 512],
                               start=(kt == 0), stop=(kt == 15))
                    return eng.matmul(PV1[0:65, :], A2[:, vb1:vb1 + 65],
                                      ering[:, slot + 512:slot + 1024],
                                      start=(kt == 0), stop=(kt == 15))
                w = [ev_exp[(P, kt)]]
                if kt == 0:
                    w += [ev_pvcopy.get(P - 1), ev_bccopy.get(P - 1)]
                if P == 0:
                    w.append(ev_vevac[kt])
                ev_pv[(P, kt)] = sch.add("t", fp, waits=w, inc=True)

            for P in range(32):
                c_, i_, tc, qch, gA, gB = unit_params(P)
                for kt in range(16):
                    reg = regs[kt % 2]
                    def fq(eng, c_=c_, kt=kt, qch=qch, tc=tc, reg=reg):
                        eng.matmul(
                            reg[:, 0:512],
                            Ksb[0:64, c_ * S + kt * 128: c_ * S + kt * 128 + 128],
                            A1[0:64, qch * TOWN + tc * 512: qch * TOWN + tc * 512 + 512],
                            start=True, stop=True)
                        return eng.matmul(
                            reg[:, 512:1024],
                            Ksb[64:128, c_ * S + kt * 128: c_ * S + kt * 128 + 128],
                            A1[64:128, qch * TOWN + tc * 512: qch * TOWN + tc * 512 + 512],
                            start=True, stop=True, tile_position=(64, 0))
                    w = [ev_exp.get((P, kt - 2), ev_exp.get((P - 1, kt + 14)))]
                    if P == 0 and kt == 0:
                        w.append(ev_kevac_last)
                    if P == 0 and kt == 1:
                        w.append(ev_vevac[15])
                    ev_qk[(P, kt)] = sch.add("t", fq, waits=w, inc=True)
                    def fe(eng, kt=kt, reg=reg):
                        return eng.activation(
                            ering[:, (kt % 8) * 1024: (kt % 8) * 1024 + 1024],
                            reg[:, :], AF.Exp, bias=cn20[:, 0:1], scale=1.0)
                    ev_exp[(P, kt)] = sch.add(
                        "a", fe, waits=[ev_qk[(P, kt)],
                                        ev_pv.get((P, kt - 8),
                                                  ev_pv.get((P - 1, kt + 8)))],
                        inc=True)
                    if kt == 0 and P > 0:
                        emit_recip(P - 1)
                    if kt == 2 and P > 0:
                        emit_norm(P - 1)
                    if kt == 3 and P > 0:
                        emit_mult(P - 1)
                    if kt >= 4:
                        emit_pv(P, kt - 4)
                    ffn.emit(1)
                for kt in (12, 13, 14, 15):
                    emit_pv(P, kt)
                    ffn.emit(1)
                def fpc(eng):
                    eng.tensor_copy(pvstage[0:65, 0:512], PV0[0:65, :])
                    return eng.tensor_copy(pvstage[0:65, 512:1024], PV1[0:65, :])
                ev_pvcopy[P] = sch.add("v", fpc,
                                       waits=[ev_pv[(P, 15)], ev_mult.get(P - 1)],
                                       inc=True)
                ffn.emit(2)
            emit_recip(31)
            emit_norm(31)
            emit_mult(31)

            ev_wodma = {}
            ev_womm = {}
            ev_woev_i = {}
            ev_woev = {}
            ev_xt = {}
            ev_pp = {}
            ev_out = {}
            ev_silu = {}
            gt_off = 8192
            xt_off = 10240
            os_off = 14336

            def emit_xt(df):
                def fx(eng, df=df):
                    return eng.dma_start(
                        out=BE[:, xt_off + (df % 2) * 1024: xt_off + (df % 2) * 1024 + 1024],
                        in_=xfull[df * 128:(df + 1) * 128, 0:TOWN])
                ev_xt[df] = sch.add("g", fx, waits=[ev_pp.get(df - 2)],
                                    inc=True, inc_n=16)

            def emit_p6(df):
                def fs(eng, df=df):
                    return eng.activation(
                        BE[:, gt_off + (df % 2) * 1024: gt_off + (df % 2) * 1024 + 1024],
                        WS[:, df * 1024:(df + 1) * 1024], AF.Silu,
                        scale=dsos[:, df:df + 1])
                ev_silu[df] = sch.add("a", fs,
                                      waits=[ffn.ev_sacc[(15, df, 1)],
                                             ev_pp.get(df - 2)], inc=True)
                def fp6(eng, df=df):
                    g = BE[:, gt_off + (df % 2) * 1024: gt_off + (df % 2) * 1024 + 1024]
                    eng.tensor_tensor(g, g, A2[:, df * TOWN:(df + 1) * TOWN],
                                      op=ALU.mult)
                    eng.drain()
                    return eng.tensor_tensor(
                        BE[:, os_off + (df % 2) * 1024: os_off + (df % 2) * 1024 + 1024],
                        BE[:, xt_off + (df % 2) * 1024: xt_off + (df % 2) * 1024 + 1024],
                        g, op=ALU.add)
                ev_pp[df] = sch.add("v", fp6,
                                    waits=[ev_silu[df], ev_woev[(df, 1)],
                                           ev_xt[df], ev_out.get(df - 2)],
                                    inc=True)
                def fo(eng, df=df):
                    return eng.dma_start(
                        out=out[df * 128:(df + 1) * 128, :],
                        in_=BE[:, os_off + (df % 2) * 1024: os_off + (df % 2) * 1024 + 1024])
                ev_out[df] = sch.add("g", fo, waits=[ev_pp[df]],
                                     inc=True, inc_n=16)
                if df + 2 < 16:
                    emit_xt(df + 2)

            def emit_wodma(w_):
                sl, hf = w_ // 2, w_ % 2
                def fw(eng, w_=w_, sl=sl, hf=hf):
                    dst = BE[:, (w_ % 2) * 4096:(w_ % 2 + 1) * 4096]
                    return eng.dma_start(
                        out=dst.rearrange("p (c j) -> p c j", j=256),
                        in_=wo_r[:, :, sl * 512 + hf * 256: sl * 512 + hf * 256 + 256])
                ww = [ev_womm.get(w_ - 2)]
                if w_ == 0:
                    ww.append(ev_pv[(31, 11)])
                if w_ == 1:
                    ww.append(ev_pv[(31, 15)])
                ev_wodma[w_] = sch.add("g", fw, waits=ww, inc=True, inc_n=16)

            emit_wodma(0)
            emit_wodma(1)
            emit_xt(0)
            emit_xt(1)
            wo_i = 0
            for w_ in range(8):
                sl, hf = w_ // 2, w_ % 2
                if w_ >= 1:
                    emit_p6(2 * (w_ - 1))
                    emit_p6(2 * (w_ - 1) + 1)
                for fsub in range(2):
                    ef = sl * 4 + hf * 2 + fsub
                    for tc2 in range(2):
                        ps = psq[wo_i % 2]
                        def fm(eng, w_=w_, fsub=fsub, tc2=tc2, ps=ps):
                            base = (w_ % 2) * 4096
                            for c in range(16):
                                ins = eng.matmul(
                                    ps,
                                    BE[:, base + c * 256 + fsub * 128: base + c * 256 + fsub * 128 + 128],
                                    A1[:, c * TOWN + tc2 * 512: c * TOWN + tc2 * 512 + 512],
                                    start=(c == 0), stop=(c == 15))
                            return ins
                        ww2 = [ev_wodma[w_], ev_woev_i.get(wo_i - 2)]
                        if wo_i == 0:
                            ww2 += [ev_mult[31], ev_aodma[31],
                                    ev_exp[(31, 15)]] + sch.dma_tokens("s")
                        if wo_i == 1:
                            ww2.append(ev_exp[(31, 15)])
                        ev_womm[w_] = sch.add("t", fm, waits=ww2, inc=True)
                        def fe(eng, ef=ef, tc2=tc2, ps=ps):
                            return eng.activation(
                                A2[:, ef * TOWN + tc2 * 512: ef * TOWN + tc2 * 512 + 512],
                                ps, AF.Identity, bias=bo_sb[:, ef:ef + 1], scale=1.0)
                        ev_woev_i[wo_i] = ev_woev[(ef, tc2)] = \
                            sch.add("a", fe, waits=[ev_womm[w_]], inc=True)
                        wo_i += 1
                ffn.emit(2)
                if w_ + 2 < 8:
                    emit_wodma(w_ + 2)
            while ffn.emit(1):
                pass
            emit_p6(14)
            emit_p6(15)
            sch.add("g", lambda eng: eng.nop(), waits=sch.dma_tokens("g"))
            sch.add("s", lambda eng: eng.nop(), waits=sch.dma_tokens("s"))
            sch.run(blk)

    return nc



def _head_perm():
    perm = np.zeros(D, dtype=np.int64)
    for c in range(4):
        for i in range(4):
            for b in range(2):
                h = 8 * c + i + 4 * b
                newbase = (4 * c + i) * 128 + b * 64
                perm[newbase:newbase + 64] = h * 64 + np.arange(64)
    return perm


def _prep_inputs(L0_attn_x, L0_attn_Wqkv, L0_attn_bqkv, L0_attn_Wo, L0_attn_bo,
                 L0_ffn_up_weight, L0_ffn_down_weight, L0_ffn_up_scale,
                 L0_ffn_down_scale, L0_ffn_output_scale):
    perm = _head_perm()
    wq = np.asarray(L0_attn_Wqkv)[:, :D][:, perm]
    wkv = np.asarray(L0_attn_Wqkv)[:, D:]
    wqkv = np.ascontiguousarray(np.concatenate([wq, wkv], axis=1), dtype=np.float32)
    bq = np.asarray(L0_attn_bqkv)[:D][perm]
    bqkv = np.ascontiguousarray(np.concatenate([bq, np.asarray(L0_attn_bqkv)[D:]]),
                                dtype=np.float32)
    wo = np.ascontiguousarray(np.asarray(L0_attn_Wo)[perm, :], dtype=np.float32)
    upt = np.ascontiguousarray(np.asarray(L0_ffn_up_weight).T, dtype=np.float32)
    dnt = np.ascontiguousarray(np.asarray(L0_ffn_down_weight).T, dtype=np.float32)
    xT = np.ascontiguousarray(np.asarray(L0_attn_x).transpose(0, 2, 1), dtype=np.float32)

    common = {
        "wqkv": wqkv, "bqkv": bqkv, "wo": wo,
        "bo": np.ascontiguousarray(np.asarray(L0_attn_bo), dtype=np.float32),
        "upt": upt, "dnt": dnt,
        "us": np.ascontiguousarray(np.asarray(L0_ffn_up_scale), dtype=np.float32),
        "ds": np.ascontiguousarray(np.asarray(L0_ffn_down_scale), dtype=np.float32),
        "osc": np.ascontiguousarray(np.asarray(L0_ffn_output_scale), dtype=np.float32),
    }
    in_maps = []
    for core in range(8):
        b, half = core // 2, core % 2
        xb = xT[b]
        if half == 0:
            xc = xb
        else:
            xc = np.concatenate([xb[:, TOWN:], xb[:, :TOWN]], axis=1)
        m = dict(common)
        m["x"] = np.ascontiguousarray(xc)
        in_maps.append(m)
    return in_maps


def kernel(**inputs):
    in_maps = _prep_inputs(**inputs)
    nc = build_program()
    res = run_bass_kernel_spmd(nc, in_maps, core_ids=list(range(8)))
    outT = np.zeros((4, D, S), dtype=np.float32)
    for core in range(8):
        b, half = core // 2, core % 2
        outT[b][:, half * TOWN:(half + 1) * TOWN] = res.results[core]["out"]
    return np.ascontiguousarray(outT.transpose(0, 2, 1))


# version 17
# speedup vs baseline: 1.2104x; 1.0108x over previous
import numpy as np
import ml_dtypes
import concourse.bass as bass
import concourse.mybir as mybir
from concourse.bass_utils import run_bass_kernel_spmd

F32 = mybir.dt.float32
BF16 = mybir.dt.bfloat16
AF = mybir.ActivationFunctionType
ALU = mybir.AluOpType

D = 2048
S = 2048
TOWN = 1024
NQ, NKV, HD = 32, 8, 64
H = 8192
SHIFT = -20.0

ENGINES = ("t", "a", "v", "g", "s")
SEMS = ("t", "a", "v", "g0", "g1", "g2", "g3", "s0", "s1", "s2", "s3")


class Sched:

    def __init__(self, nc):
        self.nc = nc
        self.sems = {}
        self.cnt = {s: 0 for s in SEMS}
        self.maxw = {}
        self.dma_idx = {"g": 0, "s": 0}
        self.last_dma_tok = {"g": {}, "s": {}}

    def open(self):
        for s in SEMS:
            self.sems[s] = self.nc.alloc_semaphore(f"sem_{s}")

    def begin_phase(self):
        self.ops = {e: [] for e in ENGINES}

    def add(self, e, fn, waits=(), inc=False, inc_n=1):
        if e in ("g", "s") and inc:
            semkey = e + str(self.dma_idx[e] % 4)
            self.dma_idx[e] += 1
        else:
            semkey = e
        w = []
        for t in waits:
            if not t:
                continue
            sk, v = t
            if v <= 0:
                continue
            if sk == e and e in ("t", "a", "v"):
                continue
            w.append((sk, v))
        self.ops[e].append((w, fn, inc, inc_n, semkey))
        if inc:
            self.cnt[semkey] += inc_n
            if e in ("g", "s"):
                self.last_dma_tok[e][semkey] = (semkey, self.cnt[semkey])
        return (semkey, self.cnt.get(semkey, 0))

    def dma_tokens(self, e):
        return list(self.last_dma_tok[e].values())

    def run(self, block):
        def runner(e):
            def go(eng):
                prev_was_dma = [False]
                for waits, fn, inc, inc_n, semkey in self.ops[e]:
                    emitted = False
                    for (sk, v) in waits:
                        key = (e, sk)
                        if self.maxw.get(key, 0) < v:
                            eng.wait_ge(self.sems[sk], v)
                            self.maxw[key] = v
                            emitted = True
                    is_dma = inc and e in ("g", "s")
                    if is_dma and prev_was_dma[0] and not emitted:
                        eng.wait_ge(self.sems[semkey],
                                    self.maxw.get((e, semkey), 0))
                    ins = fn(eng)
                    if inc:
                        ins.then_inc(self.sems[semkey], inc_n)
                    prev_was_dma[0] = is_dma
            return go

        block.tensor(runner("t"))
        block.scalar(runner("a"))
        block.vector(runner("v"))
        block.gpsimd(runner("g"))
        block.sync(runner("s"))


def build_program():
    nc = bass.Bass("TRN2", target_bir_lowering=False)

    xfull = nc.dram_tensor("x", [D, S], F32, kind="ExternalInput").ap()
    wqkv = nc.dram_tensor("wqkv", [D, 3072], F32, kind="ExternalInput").ap()
    bqkv = nc.dram_tensor("bqkv", [3072], F32, kind="ExternalInput").ap()
    wo = nc.dram_tensor("wo", [D, D], F32, kind="ExternalInput").ap()
    bo = nc.dram_tensor("bo", [D], F32, kind="ExternalInput").ap()
    upt = nc.dram_tensor("upt", [D, H], F32, kind="ExternalInput").ap()
    dnt = nc.dram_tensor("dnt", [H, D], F32, kind="ExternalInput").ap()
    us = nc.dram_tensor("us", [H], F32, kind="ExternalInput").ap()
    ds = nc.dram_tensor("ds", [D], F32, kind="ExternalInput").ap()
    osc = nc.dram_tensor("osc", [1], F32, kind="ExternalInput").ap()
    out = nc.dram_tensor("out", [D, TOWN], F32, kind="ExternalOutput").ap()

    xfull_r = xfull.rearrange("(c p) t -> p c t", p=128)
    wqkv_r = wqkv.rearrange("(c p) f -> p c f", p=128)
    wo_r = wo.rearrange("(c p) e -> p c e", p=128)
    upt_r = upt.rearrange("(c p) h -> p c h", p=128)
    dnt_r = dnt.rearrange("(c p) f -> p c f", p=128)

    sch = Sched(nc)
    sch.open()

    from contextlib import ExitStack
    with ExitStack() as _ctx:
        _e = _ctx.enter_context
        ones_bf = _e(nc.sbuf_tensor("ones_bf", [128, 1], BF16))
        onesf = _e(nc.sbuf_tensor("onesf", [128, 128], F32))
        onesb2 = _e(nc.sbuf_tensor("onesb2", [128, 128], BF16))
        c0125 = _e(nc.sbuf_tensor("c0125", [128, 1], F32))
        cn20 = _e(nc.sbuf_tensor("cn20", [128, 1], F32))
        ceps = _e(nc.sbuf_tensor("ceps", [128, 1], F32))
        cinvn = _e(nc.sbuf_tensor("cinvn", [128, 1], F32))
        bq_sb = _e(nc.sbuf_tensor("bq_sb", [128, 24], F32))
        qbias = _e(nc.sbuf_tensor("qbias", [128, 16], F32))
        bo_sb = _e(nc.sbuf_tensor("bo_sb", [128, 16], F32))
        us_sb = _e(nc.sbuf_tensor("us_sb", [128, 64], F32))
        ds_sb = _e(nc.sbuf_tensor("ds_sb", [128, 16], F32))
        dsos = _e(nc.sbuf_tensor("dsos", [128, 16], F32))
        os_sb = _e(nc.sbuf_tensor("os_sb", [1, 1], F32))
        os_col = _e(nc.sbuf_tensor("os_col", [128, 1], F32))
        vb_bc = _e(nc.sbuf_tensor("vb_bc", [128, 512], BF16))
        A1 = _e(nc.sbuf_tensor("A1", [128, 16 * TOWN], BF16))
        BE = _e(nc.sbuf_tensor("BE", [128, 16 * TOWN], BF16))
        Cxn = _e(nc.sbuf_tensor("Cxn", [128, 16 * TOWN], BF16))
        A2 = _e(nc.sbuf_tensor("A2", [128, 4 * S + 16 * 520], BF16))
        WS = _e(nc.sbuf_tensor("WS", [128, 16384], BF16))
        Hb = _e(nc.sbuf_tensor("Hb", [128, 4096], BF16))

        Ksb = A2[:, 0:4 * S]
        Vaug_off = 4 * S
        W1 = WS[:, 0:8192]
        W2 = WS[:, 8192:16384]
        ering = BE[:, 0:8192]

        ev_xl = {}
        pre_toks = []
        sch.begin_phase()
        with nc.sbuf_tensor("vb_row", [1, 512], BF16) as vb_row, \
             nc.psum_tensor("p0a", [128, 1], F32) as p0a, \
             nc.psum_tensor("p0b", [128, 512], F32) as p0b, \
             nc.Block("blk_p0") as blk:
            def f(eng):
                eng.memset(ones_bf[:, :], 1.0)
                eng.memset(onesf[:, :], 1.0)
                eng.memset(onesb2[:, :], 1.0)
                eng.memset(c0125[:, :], 0.125)
                eng.memset(cn20[:, :], SHIFT)
                eng.memset(ceps[:, :], 1e-5)
                return eng.memset(cinvn[:, :], 1.0 / D)
            ev_ms = sch.add("v", f, inc=True)

            def f(eng):
                return eng.dma_start(out=os_sb[:, :],
                                     in_=osc.rearrange("(a b) -> a b", a=1),
                                     allow_slow_non_contiguous=True)
            sch.add("s", f, inc=True, inc_n=16)
            def f(eng):
                return eng.dma_start(out=vb_row[:, :],
                                     in_=bqkv[2560:3072].rearrange("(a b) -> a b", a=1),
                                     allow_slow_non_contiguous=True)
            ev_vbl = sch.add("g", f, inc=True, inc_n=16)

            def f(eng):
                eng.matmul(p0a[:, :], onesf[0:1, :], os_sb[0:1, 0:1], start=True, stop=True)
                return eng.matmul(p0b[:, :], onesb2[0:1, :], vb_row[0:1, :], start=True, stop=True)
            ev_bc = sch.add("t", f, waits=[ev_ms, ev_vbl] + sch.dma_tokens("s"), inc=True)

            def f(eng):
                eng.tensor_copy(os_col[:, :], p0a[:, :])
                return eng.tensor_copy(vb_bc[:, :], p0b[:, :])
            ev_cp = sch.add("v", f, waits=[ev_bc], inc=True)
            sch.run(blk)

        ev_aff = {}
        ev_bcv_h1 = None
        ev_sqmm = {}
        ev_stats_prev = [None]
        with nc.sbuf_tensor("rows_f", [1, 3072], F32) as rows_f, \
             nc.sbuf_tensor("rows_b", [1, 2048], BF16) as rows_b, \
             nc.sbuf_tensor("Ff", [128, 2048], BF16) as Ff:
            r_mean = rows_f[0:1, 0:1024]
            r_var = rows_f[0:1, 1024:2048]
            r_tmp = rows_f[0:1, 2048:3072]
            r_rstd = rows_b[0:1, 0:1024]
            r_b = rows_b[0:1, 1024:2048]

            def emit_ln_stats(half, xsrc, ps_s, ps_q):
                hb_off = half * 2048
                for fc in range(16):
                    gfc = half * 16 + fc
                    def f(eng, fc=fc, xsrc=xsrc, ps_s=ps_s):
                        for n in range(2):
                            ins = eng.matmul(
                                ps_s[0:1, n * 512:(n + 1) * 512], ones_bf[:, :],
                                xsrc[:, fc * TOWN + n * 512: fc * TOWN + (n + 1) * 512],
                                start=(fc == 0), stop=(fc == 15))
                        return ins
                    sch.add("t", f, waits=[ev_xl[(half, fc // 8)], ev_stats_prev[0]], inc=True)
                    def f(eng, fc=fc, xsrc=xsrc, hb_off=hb_off):
                        return eng.activation(
                            Hb[:, hb_off + (fc % 2) * 1024: hb_off + (fc % 2) * 1024 + 1024],
                            xsrc[:, fc * TOWN:(fc + 1) * TOWN], AF.Square)
                    ev_sq = sch.add("a", f,
                                    waits=[ev_xl[(half, fc // 8)],
                                           ev_sqmm.get(gfc - 2)],
                                    inc=True)
                    def f(eng, fc=fc, hb_off=hb_off, ps_q=ps_q):
                        for n in range(2):
                            ins = eng.matmul(
                                ps_q[0:1, n * 512:(n + 1) * 512], ones_bf[:, :],
                                Hb[:, hb_off + (fc % 2) * 1024 + n * 512:
                                   hb_off + (fc % 2) * 1024 + (n + 1) * 512],
                                start=(fc == 0), stop=(fc == 15))
                        return ins
                    ev_sqmm[gfc] = sch.add("t", f,
                                           waits=[ev_sq, ev_stats_prev[0]],
                                           inc=True)
                def f(eng, ps_s=ps_s, ps_q=ps_q):
                    eng.tensor_scalar(r_mean, ps_s[0:1, :], cinvn[0:1, 0:1], None, op0=ALU.mult)
                    eng.tensor_scalar(r_var, ps_q[0:1, :], cinvn[0:1, 0:1], None, op0=ALU.mult)
                    eng.drain()
                    eng.tensor_tensor(r_tmp, r_mean, r_mean, op=ALU.mult)
                    eng.drain()
                    return eng.tensor_tensor(r_var, r_var, r_tmp, op=ALU.subtract)
                ev_st = sch.add("v", f, waits=[ev_sqmm[half * 16 + 15]], inc=True)
                ev_stats_prev[0] = ev_st
                def f(eng):
                    eng.activation(r_tmp, r_var, AF.Ln, bias=ceps[0:1, 0:1])
                    eng.drain()
                    return eng.activation(r_rstd, r_tmp, AF.Exp, scale=-0.5)
                ev_rs = sch.add("a", f, waits=[ev_st], inc=True)
                def f(eng):
                    return eng.scalar_tensor_tensor(
                        r_b, r_mean, -1.0, r_rstd, op0=ALU.mult, op1=ALU.mult)
                return sch.add("v", f, waits=[ev_rs], inc=True)

            def emit_ln_affine(half, ev_bb, xsrc, xn_dst, ps_b, ps_c):
                ev_bcv = None
                for n in range(2):
                    def f(eng, n=n, ps_b=ps_b, ps_c=ps_c):
                        eng.matmul(ps_b[:, :], onesb2[0:1, :],
                                   r_rstd[0:1, n * 512:(n + 1) * 512],
                                   start=True, stop=True)
                        return eng.matmul(ps_c[:, :], onesb2[0:1, :],
                                          r_b[0:1, n * 512:(n + 1) * 512],
                                          start=True, stop=True)
                    ev_bct0 = sch.add("t", f, waits=[ev_bb, ev_bcv], inc=True)
                    def f(eng, n=n, ps_b=ps_b, ps_c=ps_c):
                        eng.tensor_copy(Ff[:, n * 512:(n + 1) * 512], ps_b[:, :])
                        return eng.tensor_copy(Ff[:, 1024 + n * 512:1024 + (n + 1) * 512],
                                               ps_c[:, :])
                    ev_bcv = sch.add("v", f, waits=[ev_bct0], inc=True)
                for fc in range(16):
                    def f(eng, fc=fc, xsrc=xsrc, xn_dst=xn_dst):
                        if fc == 0:
                            eng.drain()
                        d = xn_dst[:, fc * TOWN:(fc + 1) * TOWN]
                        return eng.tensor_tensor(d, xsrc[:, fc * TOWN:(fc + 1) * TOWN],
                                                 Ff[:, 0:1024], op=ALU.mult)
                    sch.add("v", f, waits=[ev_bcv], inc=True)
                for fc in range(16):
                    gfc = half * 16 + fc
                    def f(eng, fc=fc, xn_dst=xn_dst):
                        d = xn_dst[:, fc * TOWN:(fc + 1) * TOWN]
                        return eng.tensor_tensor(d, d, Ff[:, 1024:2048], op=ALU.add)
                    ev_aff[gfc] = sch.add("v", f, inc=True)
                return ev_bcv

            sch.begin_phase()
            with nc.psum_tensor("p1s", [1, 1024], F32) as p1s, \
                 nc.psum_tensor("p1q", [1, 1024], F32) as p1q, \
                 nc.psum_tensor("p1b", [128, 512], F32) as p1b, \
                 nc.psum_tensor("p1c", [128, 512], F32) as p1c, \
                 nc.Block("blk_p1") as blk:
                prev = None
                def fx0(eng, pc):
                    return eng.dma_start(
                        out=A1.ap().rearrange("p (c t) -> p c t", t=TOWN)[:, pc * 8:(pc + 1) * 8, :],
                        in_=xfull_r[:, pc * 8:(pc + 1) * 8, 0:TOWN])
                def fx1(eng, pc):
                    return eng.dma_start(
                        out=BE.ap().rearrange("p (c t) -> p c t", t=TOWN)[:, pc * 8:(pc + 1) * 8, :],
                        in_=xfull_r[:, pc * 8:(pc + 1) * 8, TOWN:S])
                def fw01(eng, sl):
                    Wt = W1 if sl == 0 else W2
                    return eng.dma_start(
                        out=Wt.rearrange("p (c j) -> p c j", j=512),
                        in_=wqkv_r[:, :, sl * 512:(sl + 1) * 512])
                for kind, idx, chain in (("x0", 0, True), ("x0", 1, True),
                                         ("w", 0, True), ("w", 1, True),
                                         ("x1", 0, False), ("x1", 1, False)):
                    if kind == "x0":
                        f = (lambda eng, pc=idx: fx0(eng, pc))
                    elif kind == "x1":
                        f = (lambda eng, pc=idx: fx1(eng, pc))
                    else:
                        f = (lambda eng, sl=idx: fw01(eng, sl))
                    tok = sch.add("g", f, waits=[prev], inc=True, inc_n=16)
                    if kind == "x0":
                        ev_xl[(0, idx)] = tok
                    elif kind == "x1":
                        ev_xl[(1, idx)] = tok
                    else:
                        pre_toks.append(tok)
                    prev = tok if chain else None
                ev_vm = sch.add("v", lambda eng: eng.memset(A2[:, Vaug_off:], 1.0), inc=True)
                gath = {}
                for e, key, dst, src, c0, c1 in (
                    ("s", "bq", bq_sb, bqkv, 0, 24),
                    ("s", "us0", us_sb, us, 0, 32),
                    ("s", "bo", bo_sb, bo, 0, 16),
                    ("s", "ds", ds_sb, ds, 0, 16),
                ):
                    def f(eng, dst=dst, src=src, c0=c0, c1=c1):
                        return eng.dma_start(
                            out=dst[:, c0:c1],
                            in_=src[c0 * 128:c1 * 128].rearrange("(a b) -> b a", b=128),
                            allow_slow_non_contiguous=True)
                    gath[key] = sch.add(e, f, inc=True, inc_n=16)
                ev_bb0 = emit_ln_stats(0, A1, p1s, p1q)
                def f(eng):
                    return eng.dma_start(
                        out=us_sb[:, 32:64],
                        in_=us[32 * 128:64 * 128].rearrange("(a b) -> b a", b=128),
                        allow_slow_non_contiguous=True)
                gath["us1"] = sch.add("a", f, inc=True, inc_n=16)
                def f(eng):
                    eng.mul(qbias[:, :], bq_sb[:, 0:16], c0125[:, 0:1])
                    return eng.mul(dsos[:, :], ds_sb[:, :], os_col[:, 0:1])
                sch.add("a", f, waits=[gath["bq"], gath["ds"], ev_cp], inc=True)
                emit_ln_affine(0, ev_bb0, A1, Cxn, p1b, p1c)
                sch.run(blk)

        sch.begin_phase()
        with nc.sbuf_tensor("Wa", [128, 8192], BF16) as Wa, \
             nc.sbuf_tensor("Wb", [128, 8192], BF16) as Wb, \
             nc.sbuf_tensor("pvstage", [128, 1024], F32) as pvstage, \
             nc.psum_tensor("R0", [128, 1024], F32) as R0, \
             nc.psum_tensor("R1", [128, 1024], F32) as R1, \
             nc.psum_tensor("PVF", [128, 2048], F32) as PVF, \
             nc.Block("blk_mega") as blk:
            PV0 = PVF[:, 0:512]
            PV1 = PVF[:, 512:1024]
            FF = (PVF[:, 1024:1536], PVF[:, 1536:2048])
            regs = (R0, R1)
            ao_tmp = BE[0:64, 12288:13312]
            bc_sb = BE[:, 13312:13824]
            bc_sb2 = BE[:, 13824:14336]
            recb = BE[:, 13312:14336]

            class FStream:
                def __init__(self):
                    self.seq = []
                    for hb in range(16):
                        for hc in range(4):
                            for tcx in range(2):
                                self.seq.append(("u", hb, hc, tcx))
                        for df in range(16):
                            for tcx in range(2):
                                self.seq.append(("d", hb, df, tcx))
                    self.pos = 0
                    self.n = 0
                    self.ev_evac = {}
                    self.ev_upg = {}
                    self.ev_dng = {}
                    self.ev_signu = {}
                    self.ev_signd = {}
                    self.ev_relu = {}
                    self.ev_sacc = {}
                    self.slab_u = set()
                    self.slab_d = set()
                    self.ddma = {}
                    self.udma = {}

                def _emit_uraw(self, hb):
                    for half in (0, 1):
                        def f(eng, hb=hb, half=half):
                            return eng.dma_start(
                                out=Wa.ap().rearrange("p (c j) -> p c j", j=512)[:, half * 8:(half + 1) * 8, :],
                                in_=upt_r[:, half * 8:(half + 1) * 8, hb * 512:(hb + 1) * 512])
                        w = [self.ev_upg.get(hb - 1)]
                        if hb == 0:
                            w += [ev_aff[31], ev_bcv_h1]
                        self.udma[(hb, half)] = sch.add("g", f, waits=w,
                                                        inc=True, inc_n=16)

                def _emit_signu_chunk(self, hb, q):
                    def fs(eng, q=q):
                        reg = Wa[:, q * 2048:(q + 1) * 2048]
                        return eng.sign(reg, reg)
                    self.ev_signu[hb] = sch.add(
                        "a", fs, waits=[self.udma[(hb, q // 2)]], inc=True)

                def _emit_slab_u(self, hb):
                    self.slab_u.add(hb)
                    self._emit_uraw(hb)
                    for q in range(4):
                        self._emit_signu_chunk(hb, q)

                def _emit_slab_d(self, hb):
                    self.slab_d.add(hb)
                    for half in (0, 1):
                        def f(eng, hb=hb, half=half):
                            return eng.dma_start(
                                out=Wb.ap().rearrange("p (c j) -> p c j", j=2048)[:, half * 2:(half + 1) * 2, :],
                                in_=dnt_r[:, hb * 4 + half * 2: hb * 4 + half * 2 + 2, :])
                        w = [self.ev_dng.get(hb - 1)]
                        if hb == 0:
                            w += [ev_aff[31], ev_bcv_h1]
                        self.ddma[(hb, half)] = sch.add("g", f, waits=w,
                                                        inc=True, inc_n=16)

                def _emit_signd_chunk(self, hb, q):
                    def fs(eng, q=q):
                        reg = Wb[:, q * 2048:(q + 1) * 2048]
                        return eng.sign(reg, reg)
                    self.ev_signd[hb] = sch.add(
                        "a", fs, waits=[self.ddma[(hb, q // 2)]], inc=True)

                def emit(self, k, pred=None):
                    c = 0
                    while c < k and self.pos < len(self.seq):
                        it = self.seq[self.pos]
                        if pred is not None and not pred(it):
                            break
                        kind, hb, x, tcx = it
                        n = self.n
                        ff = FF[n % 2]
                        if kind == "u":
                            if hb not in self.slab_u:
                                self._emit_slab_u(hb)
                            if hb not in self.slab_d:
                                self._emit_slab_d(hb)
                            w = [self.ev_signu[hb], self.ev_evac.get(n - 2)]
                            if hb == 0:
                                w.append(ev_aff[15])
                            def f(eng, hc=x, tcx=tcx, ff=ff):
                                for c_ in range(16):
                                    ins = eng.matmul(
                                        ff[:, :],
                                        Wa[:, c_ * 512 + hc * 128: c_ * 512 + hc * 128 + 128],
                                        Cxn[:, c_ * TOWN + tcx * 512: c_ * TOWN + tcx * 512 + 512],
                                        start=(c_ == 0), stop=(c_ == 15))
                                return ins
                            tok = sch.add("t", f, waits=w, inc=True)
                            self.ev_upg[hb] = tok
                            def fr(eng, hb=hb, hc=x, tcx=tcx, ff=ff):
                                return eng.activation(
                                    Hb[:, hc * 1024 + tcx * 512: hc * 1024 + tcx * 512 + 512],
                                    ff[:, :], AF.Relu,
                                    scale=us_sb[:, hb * 4 + hc: hb * 4 + hc + 1])
                            rt = sch.add("a", fr, waits=[tok, self.ev_dng.get(hb - 1)], inc=True)
                            self.ev_relu[(hb, tcx)] = rt
                            self.ev_evac[n] = rt
                            gidx = x * 2 + tcx
                            if 3 <= gidx <= 6:
                                self._emit_signd_chunk(hb, gidx - 3)
                        else:
                            didx = x * 2 + tcx
                            if didx == 0 and hb + 1 < 16 and hb + 1 not in self.slab_u:
                                self.slab_u.add(hb + 1)
                                self._emit_uraw(hb + 1)
                            if didx in (24, 26, 28, 30) and hb + 1 < 16:
                                self._emit_signu_chunk(hb + 1, (didx - 24) // 2)
                            w = [self.ev_signd[hb], self.ev_relu[(hb, tcx)],
                                 self.ev_evac.get(n - 2)]
                            def f(eng, df=x, tcx=tcx, ff=ff):
                                for hc in range(4):
                                    ins = eng.matmul(
                                        ff[:, :],
                                        Wb[:, hc * 2048 + df * 128: hc * 2048 + df * 128 + 128],
                                        Hb[:, hc * 1024 + tcx * 512: hc * 1024 + tcx * 512 + 512],
                                        start=(hc == 0), stop=(hc == 3))
                                return ins
                            tok = sch.add("t", f, waits=w, inc=True)
                            self.ev_dng[hb] = tok
                            def fv(eng, hb=hb, df=x, tcx=tcx, ff=ff):
                                dst = WS[:, df * 1024 + tcx * 512: df * 1024 + tcx * 512 + 512]
                                if hb == 0:
                                    return eng.tensor_copy(dst, ff[:, :])
                                return eng.tensor_tensor(dst, dst, ff[:, :], op=ALU.add)
                            vt = sch.add("v", fv, waits=[tok], inc=True)
                            self.ev_evac[n] = vt
                            self.ev_sacc[(hb, x, tcx)] = vt
                        self.n += 1
                        self.pos += 1
                        c += 1
                    return c

            ffn = FStream()
            ev_bb1 = None

            psq = [R0[:, 0:512], R0[:, 512:1024]]
            psv = [R1[:, 0:512], R1[:, 512:1024]]
            ev_mm_sl = {}
            ev_mm_first = {}
            ev_ev = {}
            ev_vevac = {}
            ev_dma_sl = {}
            ev_kevac_last = None
            qv_i = 0
            kv_i = 0
            for sl in range(6):
                if sl == 1:
                    ev_bb1 = emit_ln_stats(1, BE, PVF[0:1, 0:1024],
                                           PVF[0:1, 1024:2048])
                if sl == 2:
                    ev_bcv_h1 = emit_ln_affine(1, ev_bb1, BE, BE,
                                               PVF[:, 0:512], PVF[:, 512:1024])
                Wb_ = W1 if sl % 2 == 0 else W2
                if sl < 2:
                    ev_dma_sl[sl] = pre_toks[sl]
                else:
                    def f(eng, sl=sl, Wb_=Wb_):
                        return eng.dma_start(
                            out=Wb_.rearrange("p (c j) -> p c j", j=512),
                            in_=wqkv_r[:, :, sl * 512:(sl + 1) * 512])
                    ev_dma_sl[sl] = sch.add("g", f,
                                            waits=[ev_mm_sl.get(sl - 2),
                                                   ev_mm_first.get(sl - 1)],
                                            inc=True, inc_n=16)
                if sl < 4:
                    for fsub in range(4):
                        qf = sl * 4 + fsub
                        for tc in range(2):
                            ps = psq[qv_i % 2]
                            def f(eng, Wb_=Wb_, fsub=fsub, tc=tc, ps=ps):
                                for c in range(16):
                                    ins = eng.matmul(
                                        ps,
                                        Wb_[:, c * 512 + fsub * 128: c * 512 + fsub * 128 + 128],
                                        Cxn[:, c * TOWN + tc * 512: c * TOWN + tc * 512 + 512],
                                        start=(c == 0), stop=(c == 15))
                                return ins
                            ev_mm = sch.add(
                                "t", f,
                                waits=[ev_dma_sl[sl], ev_aff[15],
                                       ev_ev.get(("q", qv_i % 2))],
                                inc=True)
                            ev_mm_first.setdefault(sl, ev_mm)
                            ev_mm_sl[sl] = ev_mm
                            def f(eng, qf=qf, tc=tc, ps=ps):
                                return eng.activation(
                                    A1[:, qf * TOWN + tc * 512: qf * TOWN + tc * 512 + 512],
                                    ps, AF.Identity,
                                    bias=qbias[:, qf:qf + 1], scale=0.125)
                            ev_ev[("q", qv_i % 2)] = sch.add("a", f, waits=[ev_mm], inc=True)
                            qv_i += 1
                elif sl == 4:
                    for fsub in range(4):
                        for tcf in range(4):
                            ps = psq[qv_i % 2]
                            src = Cxn if tcf < 2 else BE
                            tcl = tcf % 2
                            hh = 0 if tcf < 2 else 1
                            def f(eng, Wb_=Wb_, fsub=fsub, src=src, tcl=tcl, ps=ps):
                                for c in range(16):
                                    ins = eng.matmul(
                                        ps,
                                        Wb_[:, c * 512 + fsub * 128: c * 512 + fsub * 128 + 128],
                                        src[:, c * TOWN + tcl * 512: c * TOWN + tcl * 512 + 512],
                                        start=(c == 0), stop=(c == 15))
                                return ins
                            ev_mm = sch.add(
                                "t", f,
                                waits=[ev_dma_sl[sl], ev_aff[hh * 16 + 15],
                                       ev_ev.get(("q", qv_i % 2))],
                                inc=True)
                            ev_mm_first.setdefault(sl, ev_mm)
                            ev_mm_sl[sl] = ev_mm
                            def f(eng, fsub=fsub, tcf=tcf, ps=ps):
                                return eng.activation(
                                    Ksb[:, fsub * S + tcf * 512: fsub * S + tcf * 512 + 512],
                                    ps, AF.Identity,
                                    bias=bq_sb[:, 16 + fsub:17 + fsub], scale=1.0)
                            ev_kevac_last = ev_ev[("q", qv_i % 2)] = \
                                sch.add("a", f, waits=[ev_mm], inc=True)
                            qv_i += 1
                else:
                    for tt in range(16):
                        ps = psv[kv_i % 2]
                        src = Cxn if tt < 8 else BE
                        ttl = tt % 8
                        hh = 0 if tt < 8 else 1
                        def f(eng, Wb_=Wb_, src=src, ttl=ttl, ps=ps):
                            for c in range(16):
                                ins = eng.matmul(
                                    ps,
                                    src[:, c * TOWN + ttl * 128: c * TOWN + ttl * 128 + 128],
                                    Wb_[:, c * 512: c * 512 + 512],
                                    start=(c == 0), stop=(c == 15))
                            return ins
                        ev_mm = sch.add(
                            "t", f,
                            waits=[ev_dma_sl[sl], ev_aff[hh * 16 + 15],
                                   ev_ev.get(("v", kv_i % 2)), ev_vm, ev_bcv_h1],
                            inc=True)
                        ev_mm_first.setdefault(sl, ev_mm)
                        ev_mm_sl[sl] = ev_mm
                        def f(eng, tt=tt, ps=ps):
                            o3 = A2[:, Vaug_off + tt * 520: Vaug_off + tt * 520 + 520]
                            return eng.tensor_tensor(
                                o3.rearrange("p (g e) -> p g e", e=65)[:, :, 0:64],
                                ps.rearrange("p (g e) -> p g e", e=64),
                                vb_bc[:, :].rearrange("p (g e) -> p g e", e=64),
                                op=ALU.add)
                        ev_vevac[tt] = ev_ev[("v", kv_i % 2)] = \
                            sch.add("v", f, waits=[ev_mm], inc=True)
                        kv_i += 1
                        if tt % 2 == 1:
                            ffn.emit(1, pred=lambda it: it[0] == "u" and it[1] == 0)

            ev_qk = {}
            ev_exp = {}
            ev_pv = {}
            ev_pvcopy = {}
            ev_recip = {}
            ev_bct = {}
            ev_bccopy = {}
            ev_mult = {}
            ev_aodma = {}

            def unit_params(P):
                c_ = P // 8
                i_ = (P % 8) // 2
                tc = P % 2
                return c_, i_, tc, 4 * c_ + i_, 2 * c_, 2 * c_ + 1

            def emit_recip(P):
                def frc(eng):
                    eng.activation(pvstage[64:65, 0:1024],
                                   pvstage[64:65, 0:1024], AF.Ln)
                    eng.drain()
                    return eng.activation(recb[64:65, 0:1024],
                                          pvstage[64:65, 0:1024], AF.Exp, scale=-1.0)
                ev_recip[P] = sch.add("a", frc,
                                      waits=[ev_pvcopy[P], ev_bct.get(P - 1)],
                                      inc=True)

            def emit_norm(P):
                def fb(eng):
                    eng.matmul(PV0[0:64, :], onesb2[64:65, 0:64],
                               recb[64:65, 0:512],
                               start=True, stop=True, tile_position=(64, 0))
                    return eng.matmul(PV1[0:64, :], onesb2[64:65, 0:64],
                                      recb[64:65, 512:1024],
                                      start=True, stop=True, tile_position=(64, 0))
                ev_bct[P] = sch.add("t", fb, waits=[ev_recip[P]], inc=True)
                def fc(eng):
                    eng.tensor_copy(bc_sb[0:64, :], PV0[0:64, :])
                    return eng.tensor_copy(bc_sb2[0:64, :], PV1[0:64, :])
                ev_bccopy[P] = sch.add("v", fc, waits=[ev_bct[P]], inc=True)

            def emit_mult(P):
                c_, i_, tc, qch, gA, gB = unit_params(P)
                slot = qch * TOWN + tc * 512
                def fm(eng, P=P, slot=slot):
                    eng.tensor_tensor(A1[0:64, slot:slot + 512],
                                      pvstage[0:64, 0:512], bc_sb[0:64, :],
                                      op=ALU.mult)
                    return eng.tensor_tensor(
                        ao_tmp[0:64, (P % 2) * 512:(P % 2) * 512 + 512],
                        pvstage[0:64, 512:1024], bc_sb2[0:64, :], op=ALU.mult)
                ev_mult[P] = sch.add("v", fm,
                                     waits=[ev_bccopy[P], ev_aodma.get(P - 2)],
                                     inc=True)
                def fd(eng, P=P, slot=slot):
                    return eng.dma_start(
                        out=A1[64:128, slot:slot + 512],
                        in_=ao_tmp[0:64, (P % 2) * 512:(P % 2) * 512 + 512])
                ev_aodma[P] = sch.add("s", fd, waits=[ev_mult[P]],
                                      inc=True, inc_n=16)

            def emit_pv(P, kt):
                c_, i_, tc, qch, gA, gB = unit_params(P)
                slot = (kt % 8) * 1024
                vb0 = Vaug_off + kt * 520 + gA * 65
                vb1 = Vaug_off + kt * 520 + gB * 65
                def fp(eng, kt=kt, slot=slot, vb0=vb0, vb1=vb1):
                    eng.matmul(PV0[0:65, :], A2[:, vb0:vb0 + 65],
                               ering[:, slot:slot + 512],
                               start=(kt == 0), stop=(kt == 15))
                    return eng.matmul(PV1[0:65, :], A2[:, vb1:vb1 + 65],
                                      ering[:, slot + 512:slot + 1024],
                                      start=(kt == 0), stop=(kt == 15))
                w = [ev_exp[(P, kt)]]
                if kt == 0:
                    w += [ev_pvcopy.get(P - 1), ev_bccopy.get(P - 1)]
                if P == 0:
                    w.append(ev_vevac[kt])
                ev_pv[(P, kt)] = sch.add("t", fp, waits=w, inc=True)

            for P in range(32):
                c_, i_, tc, qch, gA, gB = unit_params(P)
                for kt in range(16):
                    reg = regs[kt % 2]
                    def fq(eng, c_=c_, kt=kt, qch=qch, tc=tc, reg=reg):
                        eng.matmul(
                            reg[:, 0:512],
                            Ksb[0:64, c_ * S + kt * 128: c_ * S + kt * 128 + 128],
                            A1[0:64, qch * TOWN + tc * 512: qch * TOWN + tc * 512 + 512],
                            start=True, stop=True)
                        return eng.matmul(
                            reg[:, 512:1024],
                            Ksb[64:128, c_ * S + kt * 128: c_ * S + kt * 128 + 128],
                            A1[64:128, qch * TOWN + tc * 512: qch * TOWN + tc * 512 + 512],
                            start=True, stop=True, tile_position=(64, 0))
                    w = [ev_exp.get((P, kt - 2), ev_exp.get((P - 1, kt + 14)))]
                    if P == 0 and kt == 0:
                        w.append(ev_kevac_last)
                    if P == 0 and kt == 1:
                        w.append(ev_vevac[15])
                    ev_qk[(P, kt)] = sch.add("t", fq, waits=w, inc=True)
                    def fe(eng, kt=kt, reg=reg):
                        return eng.activation(
                            ering[:, (kt % 8) * 1024: (kt % 8) * 1024 + 1024],
                            reg[:, :], AF.Exp, bias=cn20[:, 0:1], scale=1.0)
                    ev_exp[(P, kt)] = sch.add(
                        "a", fe, waits=[ev_qk[(P, kt)],
                                        ev_pv.get((P, kt - 8),
                                                  ev_pv.get((P - 1, kt + 8)))],
                        inc=True)
                    if kt == 0 and P > 0:
                        emit_recip(P - 1)
                    if kt == 2 and P > 0:
                        emit_norm(P - 1)
                    if kt == 3 and P > 0:
                        emit_mult(P - 1)
                    if kt >= 4:
                        emit_pv(P, kt - 4)
                    ffn.emit(1)
                for kt in (12, 13, 14, 15):
                    emit_pv(P, kt)
                    ffn.emit(1)
                def fpc(eng):
                    eng.tensor_copy(pvstage[0:65, 0:512], PV0[0:65, :])
                    return eng.tensor_copy(pvstage[0:65, 512:1024], PV1[0:65, :])
                ev_pvcopy[P] = sch.add("v", fpc,
                                       waits=[ev_pv[(P, 15)], ev_mult.get(P - 1)],
                                       inc=True)
                ffn.emit(2)
            emit_recip(31)
            emit_norm(31)
            emit_mult(31)

            ev_wodma = {}
            ev_womm = {}
            ev_woev_i = {}
            ev_woev = {}
            ev_xt = {}
            ev_pp = {}
            ev_out = {}
            ev_silu = {}
            gt_off = 8192
            xt_off = 10240
            os_off = 14336

            def emit_xt(df):
                def fx(eng, df=df):
                    return eng.dma_start(
                        out=BE[:, xt_off + (df % 2) * 1024: xt_off + (df % 2) * 1024 + 1024],
                        in_=xfull[df * 128:(df + 1) * 128, 0:TOWN])
                ev_xt[df] = sch.add("g", fx, waits=[ev_pp.get(df - 2)],
                                    inc=True, inc_n=16)

            def emit_p6(df):
                def fs(eng, df=df):
                    return eng.activation(
                        BE[:, gt_off + (df % 2) * 1024: gt_off + (df % 2) * 1024 + 1024],
                        WS[:, df * 1024:(df + 1) * 1024], AF.Silu,
                        scale=dsos[:, df:df + 1])
                ev_silu[df] = sch.add("a", fs,
                                      waits=[ffn.ev_sacc[(15, df, 1)],
                                             ev_pp.get(df - 2)], inc=True)
                def fp6(eng, df=df):
                    g = BE[:, gt_off + (df % 2) * 1024: gt_off + (df % 2) * 1024 + 1024]
                    eng.tensor_tensor(g, g, A2[:, df * TOWN:(df + 1) * TOWN],
                                      op=ALU.mult)
                    eng.drain()
                    return eng.tensor_tensor(
                        BE[:, os_off + (df % 2) * 1024: os_off + (df % 2) * 1024 + 1024],
                        BE[:, xt_off + (df % 2) * 1024: xt_off + (df % 2) * 1024 + 1024],
                        g, op=ALU.add)
                ev_pp[df] = sch.add("v", fp6,
                                    waits=[ev_silu[df], ev_woev[(df, 1)],
                                           ev_xt[df], ev_out.get(df - 2)],
                                    inc=True)
                def fo(eng, df=df):
                    return eng.dma_start(
                        out=out[df * 128:(df + 1) * 128, :],
                        in_=BE[:, os_off + (df % 2) * 1024: os_off + (df % 2) * 1024 + 1024])
                ev_out[df] = sch.add("g", fo, waits=[ev_pp[df]],
                                     inc=True, inc_n=16)
                if df + 2 < 16:
                    emit_xt(df + 2)

            def emit_wodma(w_):
                sl, hf = w_ // 2, w_ % 2
                def fw(eng, w_=w_, sl=sl, hf=hf):
                    dst = BE[:, (w_ % 2) * 4096:(w_ % 2 + 1) * 4096]
                    return eng.dma_start(
                        out=dst.rearrange("p (c j) -> p c j", j=256),
                        in_=wo_r[:, :, sl * 512 + hf * 256: sl * 512 + hf * 256 + 256])
                ww = [ev_womm.get(w_ - 2)]
                if w_ == 0:
                    ww.append(ev_pv[(31, 11)])
                if w_ == 1:
                    ww.append(ev_pv[(31, 15)])
                ev_wodma[w_] = sch.add("g", fw, waits=ww, inc=True, inc_n=16)

            emit_wodma(0)
            emit_wodma(1)
            emit_xt(0)
            emit_xt(1)
            wo_i = 0
            for w_ in range(8):
                sl, hf = w_ // 2, w_ % 2
                if w_ >= 1:
                    emit_p6(2 * (w_ - 1))
                    emit_p6(2 * (w_ - 1) + 1)
                for fsub in range(2):
                    ef = sl * 4 + hf * 2 + fsub
                    for tc2 in range(2):
                        ps = psq[wo_i % 2]
                        def fm(eng, w_=w_, fsub=fsub, tc2=tc2, ps=ps):
                            base = (w_ % 2) * 4096
                            for c in range(16):
                                ins = eng.matmul(
                                    ps,
                                    BE[:, base + c * 256 + fsub * 128: base + c * 256 + fsub * 128 + 128],
                                    A1[:, c * TOWN + tc2 * 512: c * TOWN + tc2 * 512 + 512],
                                    start=(c == 0), stop=(c == 15))
                            return ins
                        ww2 = [ev_wodma[w_], ev_woev_i.get(wo_i - 2)]
                        if wo_i == 0:
                            ww2 += [ev_mult[31], ev_aodma[31],
                                    ev_exp[(31, 15)]] + sch.dma_tokens("s")
                        if wo_i == 1:
                            ww2.append(ev_exp[(31, 15)])
                        ev_womm[w_] = sch.add("t", fm, waits=ww2, inc=True)
                        def fe(eng, ef=ef, tc2=tc2, ps=ps):
                            return eng.activation(
                                A2[:, ef * TOWN + tc2 * 512: ef * TOWN + tc2 * 512 + 512],
                                ps, AF.Identity, bias=bo_sb[:, ef:ef + 1], scale=1.0)
                        ev_woev_i[wo_i] = ev_woev[(ef, tc2)] = \
                            sch.add("a", fe, waits=[ev_womm[w_]], inc=True)
                        wo_i += 1
                ffn.emit(2)
                if w_ + 2 < 8:
                    emit_wodma(w_ + 2)
            while ffn.emit(1):
                pass
            emit_p6(14)
            emit_p6(15)
            sch.add("g", lambda eng: eng.nop(), waits=sch.dma_tokens("g"))
            sch.add("s", lambda eng: eng.nop(), waits=sch.dma_tokens("s"))
            sch.run(blk)

    return nc



def _head_perm():
    perm = np.zeros(D, dtype=np.int64)
    for c in range(4):
        for i in range(4):
            for b in range(2):
                h = 8 * c + i + 4 * b
                newbase = (4 * c + i) * 128 + b * 64
                perm[newbase:newbase + 64] = h * 64 + np.arange(64)
    return perm


def _prep_inputs(L0_attn_x, L0_attn_Wqkv, L0_attn_bqkv, L0_attn_Wo, L0_attn_bo,
                 L0_ffn_up_weight, L0_ffn_down_weight, L0_ffn_up_scale,
                 L0_ffn_down_scale, L0_ffn_output_scale):
    perm = _head_perm()
    wq = np.asarray(L0_attn_Wqkv)[:, :D][:, perm]
    wkv = np.asarray(L0_attn_Wqkv)[:, D:]
    wqkv = np.ascontiguousarray(np.concatenate([wq, wkv], axis=1), dtype=np.float32)
    bq = np.asarray(L0_attn_bqkv)[:D][perm]
    bqkv = np.ascontiguousarray(np.concatenate([bq, np.asarray(L0_attn_bqkv)[D:]]),
                                dtype=np.float32)
    wo = np.ascontiguousarray(np.asarray(L0_attn_Wo)[perm, :], dtype=np.float32)
    upt = np.ascontiguousarray(np.asarray(L0_ffn_up_weight).T, dtype=np.float32)
    dnt = np.ascontiguousarray(np.asarray(L0_ffn_down_weight).T, dtype=np.float32)
    xT = np.ascontiguousarray(np.asarray(L0_attn_x).transpose(0, 2, 1), dtype=np.float32)

    common = {
        "wqkv": wqkv, "bqkv": bqkv, "wo": wo,
        "bo": np.ascontiguousarray(np.asarray(L0_attn_bo), dtype=np.float32),
        "upt": upt, "dnt": dnt,
        "us": np.ascontiguousarray(np.asarray(L0_ffn_up_scale), dtype=np.float32),
        "ds": np.ascontiguousarray(np.asarray(L0_ffn_down_scale), dtype=np.float32),
        "osc": np.ascontiguousarray(np.asarray(L0_ffn_output_scale), dtype=np.float32),
    }
    in_maps = []
    for core in range(8):
        b, half = core // 2, core % 2
        xb = xT[b]
        if half == 0:
            xc = xb
        else:
            xc = np.concatenate([xb[:, TOWN:], xb[:, :TOWN]], axis=1)
        m = dict(common)
        m["x"] = np.ascontiguousarray(xc)
        in_maps.append(m)
    return in_maps


def kernel(**inputs):
    in_maps = _prep_inputs(**inputs)
    nc = build_program()
    res = run_bass_kernel_spmd(nc, in_maps, core_ids=list(range(8)))
    outT = np.zeros((4, D, S), dtype=np.float32)
    for core in range(8):
        b, half = core // 2, core % 2
        outT[b][:, half * TOWN:(half + 1) * TOWN] = res.results[core]["out"]
    return np.ascontiguousarray(outT.transpose(0, 2, 1))


# version 18
# speedup vs baseline: 1.2245x; 1.0117x over previous
import numpy as np
import ml_dtypes
import concourse.bass as bass
import concourse.mybir as mybir
from concourse.bass_utils import run_bass_kernel_spmd

F32 = mybir.dt.float32
BF16 = mybir.dt.bfloat16
AF = mybir.ActivationFunctionType
ALU = mybir.AluOpType

D = 2048
S = 2048
TOWN = 1024
NQ, NKV, HD = 32, 8, 64
H = 8192
SHIFT = -20.0

ENGINES = ("t", "a", "v", "g", "s")
SEMS = ("t", "a", "v", "g0", "g1", "g2", "g3", "s0", "s1", "s2", "s3")


class Sched:

    def __init__(self, nc):
        self.nc = nc
        self.sems = {}
        self.cnt = {s: 0 for s in SEMS}
        self.maxw = {}
        self.dma_idx = {"g": 0, "s": 0}
        self.last_dma_tok = {"g": {}, "s": {}}

    def open(self):
        for s in SEMS:
            self.sems[s] = self.nc.alloc_semaphore(f"sem_{s}")

    def begin_phase(self):
        self.ops = {e: [] for e in ENGINES}

    def add(self, e, fn, waits=(), inc=False, inc_n=1):
        if e in ("g", "s") and inc:
            semkey = e + str(self.dma_idx[e] % 4)
            self.dma_idx[e] += 1
        else:
            semkey = e
        w = []
        for t in waits:
            if not t:
                continue
            sk, v = t
            if v <= 0:
                continue
            if sk == e and e in ("t", "a", "v"):
                continue
            w.append((sk, v))
        self.ops[e].append((w, fn, inc, inc_n, semkey))
        if inc:
            self.cnt[semkey] += inc_n
            if e in ("g", "s"):
                self.last_dma_tok[e][semkey] = (semkey, self.cnt[semkey])
        return (semkey, self.cnt.get(semkey, 0))

    def dma_tokens(self, e):
        return list(self.last_dma_tok[e].values())

    def run(self, block):
        def runner(e):
            def go(eng):
                prev_was_dma = [False]
                for waits, fn, inc, inc_n, semkey in self.ops[e]:
                    emitted = False
                    for (sk, v) in waits:
                        key = (e, sk)
                        if self.maxw.get(key, 0) < v:
                            eng.wait_ge(self.sems[sk], v)
                            self.maxw[key] = v
                            emitted = True
                    is_dma = inc and e in ("g", "s")
                    if is_dma and prev_was_dma[0] and not emitted:
                        eng.wait_ge(self.sems[semkey],
                                    self.maxw.get((e, semkey), 0))
                    ins = fn(eng)
                    if inc:
                        ins.then_inc(self.sems[semkey], inc_n)
                    prev_was_dma[0] = is_dma
            return go

        block.tensor(runner("t"))
        block.scalar(runner("a"))
        block.vector(runner("v"))
        block.gpsimd(runner("g"))
        block.sync(runner("s"))


def build_program():
    nc = bass.Bass("TRN2", target_bir_lowering=False)

    xfull = nc.dram_tensor("x", [D, S], F32, kind="ExternalInput").ap()
    wqkv = nc.dram_tensor("wqkv", [D, 3072], F32, kind="ExternalInput").ap()
    bqkv = nc.dram_tensor("bqkv", [3072], F32, kind="ExternalInput").ap()
    wo = nc.dram_tensor("wo", [D, D], F32, kind="ExternalInput").ap()
    bo = nc.dram_tensor("bo", [D], F32, kind="ExternalInput").ap()
    upt = nc.dram_tensor("upt", [D, H], F32, kind="ExternalInput").ap()
    dnt = nc.dram_tensor("dnt", [H, D], F32, kind="ExternalInput").ap()
    us = nc.dram_tensor("us", [H], F32, kind="ExternalInput").ap()
    ds = nc.dram_tensor("ds", [D], F32, kind="ExternalInput").ap()
    osc = nc.dram_tensor("osc", [1], F32, kind="ExternalInput").ap()
    out = nc.dram_tensor("out", [D, TOWN], F32, kind="ExternalOutput").ap()

    xfull_r = xfull.rearrange("(c p) t -> p c t", p=128)
    wqkv_r = wqkv.rearrange("(c p) f -> p c f", p=128)
    wo_r = wo.rearrange("(c p) e -> p c e", p=128)
    upt_r = upt.rearrange("(c p) h -> p c h", p=128)
    dnt_r = dnt.rearrange("(c p) f -> p c f", p=128)

    sch = Sched(nc)
    sch.open()

    from contextlib import ExitStack
    with ExitStack() as _ctx:
        _e = _ctx.enter_context
        ones_bf = _e(nc.sbuf_tensor("ones_bf", [128, 1], BF16))
        onesf = _e(nc.sbuf_tensor("onesf", [128, 128], F32))
        onesb2 = _e(nc.sbuf_tensor("onesb2", [128, 128], BF16))
        c0125 = _e(nc.sbuf_tensor("c0125", [128, 1], F32))
        cn20 = _e(nc.sbuf_tensor("cn20", [128, 1], F32))
        ceps = _e(nc.sbuf_tensor("ceps", [128, 1], F32))
        cinvn = _e(nc.sbuf_tensor("cinvn", [128, 1], F32))
        bq_sb = _e(nc.sbuf_tensor("bq_sb", [128, 24], F32))
        qbias = _e(nc.sbuf_tensor("qbias", [128, 16], F32))
        bo_sb = _e(nc.sbuf_tensor("bo_sb", [128, 16], F32))
        us_sb = _e(nc.sbuf_tensor("us_sb", [128, 64], F32))
        ds_sb = _e(nc.sbuf_tensor("ds_sb", [128, 16], F32))
        dsos = _e(nc.sbuf_tensor("dsos", [128, 16], F32))
        os_sb = _e(nc.sbuf_tensor("os_sb", [1, 1], F32))
        os_col = _e(nc.sbuf_tensor("os_col", [128, 1], F32))
        vb_bc = _e(nc.sbuf_tensor("vb_bc", [128, 512], BF16))
        A1 = _e(nc.sbuf_tensor("A1", [128, 16 * TOWN], BF16))
        BE = _e(nc.sbuf_tensor("BE", [128, 16 * TOWN], BF16))
        Cxn = _e(nc.sbuf_tensor("Cxn", [128, 16 * TOWN], BF16))
        A2 = _e(nc.sbuf_tensor("A2", [128, 4 * S + 16 * 520], BF16))
        WS = _e(nc.sbuf_tensor("WS", [128, 16384], BF16))
        Hb = _e(nc.sbuf_tensor("Hb", [128, 4096], BF16))

        Ksb = A2[:, 0:4 * S]
        Vaug_off = 4 * S
        W1 = WS[:, 0:8192]
        W2 = WS[:, 8192:16384]
        ering = BE[:, 0:8192]

        ev_xl = {}
        pre_toks = []
        sch.begin_phase()
        with nc.sbuf_tensor("vb_row", [1, 512], BF16) as vb_row, \
             nc.psum_tensor("p0a", [128, 1], F32) as p0a, \
             nc.psum_tensor("p0b", [128, 512], F32) as p0b, \
             nc.Block("blk_p0") as blk:
            def f(eng):
                eng.memset(ones_bf[:, :], 1.0)
                eng.memset(onesf[:, :], 1.0)
                eng.memset(onesb2[:, :], 1.0)
                eng.memset(c0125[:, :], 0.125)
                eng.memset(cn20[:, :], SHIFT)
                eng.memset(ceps[:, :], 1e-5)
                return eng.memset(cinvn[:, :], 1.0 / D)
            ev_ms = sch.add("v", f, inc=True)

            def f(eng):
                return eng.dma_start(out=os_sb[:, :],
                                     in_=osc.rearrange("(a b) -> a b", a=1),
                                     allow_slow_non_contiguous=True)
            sch.add("s", f, inc=True, inc_n=16)
            def f(eng):
                return eng.dma_start(out=vb_row[:, :],
                                     in_=bqkv[2560:3072].rearrange("(a b) -> a b", a=1),
                                     allow_slow_non_contiguous=True)
            ev_vbl = sch.add("g", f, inc=True, inc_n=16)

            def f(eng):
                eng.matmul(p0a[:, :], onesf[0:1, :], os_sb[0:1, 0:1], start=True, stop=True)
                return eng.matmul(p0b[:, :], onesb2[0:1, :], vb_row[0:1, :], start=True, stop=True)
            ev_bc = sch.add("t", f, waits=[ev_ms, ev_vbl] + sch.dma_tokens("s"), inc=True)

            def f(eng):
                eng.tensor_copy(os_col[:, :], p0a[:, :])
                return eng.tensor_copy(vb_bc[:, :], p0b[:, :])
            ev_cp = sch.add("v", f, waits=[ev_bc], inc=True)
            sch.run(blk)

        ev_aff = {}
        ev_bcv_h1 = None
        ev_sqmm = {}
        ev_stats_prev = [None]
        with nc.sbuf_tensor("rows_f", [1, 3072], F32) as rows_f, \
             nc.sbuf_tensor("rows_b", [1, 2048], BF16) as rows_b, \
             nc.sbuf_tensor("Ff", [128, 2048], BF16) as Ff:
            r_mean = rows_f[0:1, 0:1024]
            r_var = rows_f[0:1, 1024:2048]
            r_tmp = rows_f[0:1, 2048:3072]
            r_rstd = rows_b[0:1, 0:1024]
            r_b = rows_b[0:1, 1024:2048]

            def emit_ln_stats(half, xsrc, ps_s, ps_q):
                hb_off = half * 2048
                for fc in range(16):
                    gfc = half * 16 + fc
                    def f(eng, fc=fc, xsrc=xsrc, ps_s=ps_s):
                        for n in range(2):
                            ins = eng.matmul(
                                ps_s[0:1, n * 512:(n + 1) * 512], ones_bf[:, :],
                                xsrc[:, fc * TOWN + n * 512: fc * TOWN + (n + 1) * 512],
                                start=(fc == 0), stop=(fc == 15))
                        return ins
                    sch.add("t", f, waits=[ev_xl[(half, fc // 8)], ev_stats_prev[0]], inc=True)
                    def f(eng, fc=fc, xsrc=xsrc, hb_off=hb_off):
                        return eng.activation(
                            Hb[:, hb_off + (fc % 2) * 1024: hb_off + (fc % 2) * 1024 + 1024],
                            xsrc[:, fc * TOWN:(fc + 1) * TOWN], AF.Square)
                    ev_sq = sch.add("a", f,
                                    waits=[ev_xl[(half, fc // 8)],
                                           ev_sqmm.get(gfc - 2)],
                                    inc=True)
                    def f(eng, fc=fc, hb_off=hb_off, ps_q=ps_q):
                        for n in range(2):
                            ins = eng.matmul(
                                ps_q[0:1, n * 512:(n + 1) * 512], ones_bf[:, :],
                                Hb[:, hb_off + (fc % 2) * 1024 + n * 512:
                                   hb_off + (fc % 2) * 1024 + (n + 1) * 512],
                                start=(fc == 0), stop=(fc == 15))
                        return ins
                    ev_sqmm[gfc] = sch.add("t", f,
                                           waits=[ev_sq, ev_stats_prev[0]],
                                           inc=True)
                def f(eng, ps_s=ps_s, ps_q=ps_q):
                    eng.tensor_scalar(r_mean, ps_s[0:1, :], cinvn[0:1, 0:1], None, op0=ALU.mult)
                    eng.tensor_scalar(r_var, ps_q[0:1, :], cinvn[0:1, 0:1], None, op0=ALU.mult)
                    eng.drain()
                    eng.tensor_tensor(r_tmp, r_mean, r_mean, op=ALU.mult)
                    eng.drain()
                    return eng.tensor_tensor(r_var, r_var, r_tmp, op=ALU.subtract)
                ev_st = sch.add("v", f, waits=[ev_sqmm[half * 16 + 15]], inc=True)
                ev_stats_prev[0] = ev_st
                def f(eng):
                    eng.activation(r_tmp, r_var, AF.Ln, bias=ceps[0:1, 0:1])
                    eng.drain()
                    return eng.activation(r_rstd, r_tmp, AF.Exp, scale=-0.5)
                ev_rs = sch.add("a", f, waits=[ev_st], inc=True)
                def f(eng):
                    return eng.scalar_tensor_tensor(
                        r_b, r_mean, -1.0, r_rstd, op0=ALU.mult, op1=ALU.mult)
                return sch.add("v", f, waits=[ev_rs], inc=True)

            def emit_ln_affine(half, ev_bb, xsrc, xn_dst, ps_b, ps_c):
                ev_bcv = None
                for n in range(2):
                    def f(eng, n=n, ps_b=ps_b, ps_c=ps_c):
                        eng.matmul(ps_b[:, :], onesb2[0:1, :],
                                   r_rstd[0:1, n * 512:(n + 1) * 512],
                                   start=True, stop=True)
                        return eng.matmul(ps_c[:, :], onesb2[0:1, :],
                                          r_b[0:1, n * 512:(n + 1) * 512],
                                          start=True, stop=True)
                    ev_bct0 = sch.add("t", f, waits=[ev_bb, ev_bcv], inc=True)
                    def f(eng, n=n, ps_b=ps_b, ps_c=ps_c):
                        eng.tensor_copy(Ff[:, n * 512:(n + 1) * 512], ps_b[:, :])
                        return eng.tensor_copy(Ff[:, 1024 + n * 512:1024 + (n + 1) * 512],
                                               ps_c[:, :])
                    ev_bcv = sch.add("v", f, waits=[ev_bct0], inc=True)
                for fc in range(16):
                    def f(eng, fc=fc, xsrc=xsrc, xn_dst=xn_dst):
                        if fc == 0:
                            eng.drain()
                        d = xn_dst[:, fc * TOWN:(fc + 1) * TOWN]
                        return eng.tensor_tensor(d, xsrc[:, fc * TOWN:(fc + 1) * TOWN],
                                                 Ff[:, 0:1024], op=ALU.mult)
                    sch.add("v", f, waits=[ev_bcv], inc=True)
                for fc in range(16):
                    gfc = half * 16 + fc
                    def f(eng, fc=fc, xn_dst=xn_dst):
                        d = xn_dst[:, fc * TOWN:(fc + 1) * TOWN]
                        return eng.tensor_tensor(d, d, Ff[:, 1024:2048], op=ALU.add)
                    ev_aff[gfc] = sch.add("v", f, inc=True)
                return ev_bcv

            sch.begin_phase()
            with nc.psum_tensor("p1s", [1, 1024], F32) as p1s, \
                 nc.psum_tensor("p1q", [1, 1024], F32) as p1q, \
                 nc.psum_tensor("p1b", [128, 512], F32) as p1b, \
                 nc.psum_tensor("p1c", [128, 512], F32) as p1c, \
                 nc.Block("blk_p1") as blk:
                prev = None
                def fx0(eng, pc):
                    return eng.dma_start(
                        out=A1.ap().rearrange("p (c t) -> p c t", t=TOWN)[:, pc * 8:(pc + 1) * 8, :],
                        in_=xfull_r[:, pc * 8:(pc + 1) * 8, 0:TOWN])
                def fx1(eng, pc):
                    return eng.dma_start(
                        out=BE.ap().rearrange("p (c t) -> p c t", t=TOWN)[:, pc * 8:(pc + 1) * 8, :],
                        in_=xfull_r[:, pc * 8:(pc + 1) * 8, TOWN:S])
                def fw01(eng, sl):
                    Wt = W1 if sl == 0 else W2
                    return eng.dma_start(
                        out=Wt.rearrange("p (c j) -> p c j", j=512),
                        in_=wqkv_r[:, :, sl * 512:(sl + 1) * 512])
                for kind, idx, chain in (("x0", 0, False), ("x0", 1, False),
                                         ("w", 0, False), ("w", 1, False),
                                         ("x1", 0, False), ("x1", 1, False)):
                    if kind == "x0":
                        f = (lambda eng, pc=idx: fx0(eng, pc))
                    elif kind == "x1":
                        f = (lambda eng, pc=idx: fx1(eng, pc))
                    else:
                        f = (lambda eng, sl=idx: fw01(eng, sl))
                    tok = sch.add("g", f, waits=[prev], inc=True, inc_n=16)
                    if kind == "x0":
                        ev_xl[(0, idx)] = tok
                    elif kind == "x1":
                        ev_xl[(1, idx)] = tok
                    else:
                        pre_toks.append(tok)
                    prev = tok if chain else None
                ev_vm = sch.add("v", lambda eng: eng.memset(A2[:, Vaug_off:], 1.0), inc=True)
                gath = {}
                for e, key, dst, src, c0, c1 in (
                    ("s", "bq", bq_sb, bqkv, 0, 24),
                    ("s", "us0", us_sb, us, 0, 32),
                    ("s", "bo", bo_sb, bo, 0, 16),
                    ("s", "ds", ds_sb, ds, 0, 16),
                ):
                    def f(eng, dst=dst, src=src, c0=c0, c1=c1):
                        return eng.dma_start(
                            out=dst[:, c0:c1],
                            in_=src[c0 * 128:c1 * 128].rearrange("(a b) -> b a", b=128),
                            allow_slow_non_contiguous=True)
                    gath[key] = sch.add(e, f, inc=True, inc_n=16)
                ev_bb0 = emit_ln_stats(0, A1, p1s, p1q)
                def f(eng):
                    return eng.dma_start(
                        out=us_sb[:, 32:64],
                        in_=us[32 * 128:64 * 128].rearrange("(a b) -> b a", b=128),
                        allow_slow_non_contiguous=True)
                gath["us1"] = sch.add("a", f, inc=True, inc_n=16)
                def f(eng):
                    eng.mul(qbias[:, :], bq_sb[:, 0:16], c0125[:, 0:1])
                    return eng.mul(dsos[:, :], ds_sb[:, :], os_col[:, 0:1])
                sch.add("a", f, waits=[gath["bq"], gath["ds"], ev_cp], inc=True)
                emit_ln_affine(0, ev_bb0, A1, Cxn, p1b, p1c)
                sch.run(blk)

        sch.begin_phase()
        with nc.sbuf_tensor("Wa", [128, 8192], BF16) as Wa, \
             nc.sbuf_tensor("Wb", [128, 8192], BF16) as Wb, \
             nc.sbuf_tensor("pvstage", [128, 1024], F32) as pvstage, \
             nc.psum_tensor("R0", [128, 1024], F32) as R0, \
             nc.psum_tensor("R1", [128, 1024], F32) as R1, \
             nc.psum_tensor("PVF", [128, 2048], F32) as PVF, \
             nc.Block("blk_mega") as blk:
            PV0 = PVF[:, 0:512]
            PV1 = PVF[:, 512:1024]
            FF = (PVF[:, 1024:1536], PVF[:, 1536:2048])
            regs = (R0, R1)
            ao_tmp = BE[0:64, 12288:13312]
            bc_sb = BE[:, 13312:13824]
            bc_sb2 = BE[:, 13824:14336]
            recb = BE[:, 13312:14336]

            class FStream:
                def __init__(self):
                    self.seq = []
                    for hb in range(16):
                        for hc in range(4):
                            for tcx in range(2):
                                self.seq.append(("u", hb, hc, tcx))
                        for df in range(16):
                            for tcx in range(2):
                                self.seq.append(("d", hb, df, tcx))
                    self.pos = 0
                    self.n = 0
                    self.ev_evac = {}
                    self.ev_upg = {}
                    self.ev_dng = {}
                    self.ev_signu = {}
                    self.ev_signd = {}
                    self.ev_relu = {}
                    self.ev_sacc = {}
                    self.slab_u = set()
                    self.slab_d = set()
                    self.ddma = {}
                    self.udma = {}

                def _emit_uraw(self, hb):
                    for half in (0, 1):
                        def f(eng, hb=hb, half=half):
                            return eng.dma_start(
                                out=Wa.ap().rearrange("p (c j) -> p c j", j=512)[:, half * 8:(half + 1) * 8, :],
                                in_=upt_r[:, half * 8:(half + 1) * 8, hb * 512:(hb + 1) * 512])
                        w = [self.ev_upg.get(hb - 1)]
                        if hb == 0:
                            w += [ev_aff[31], ev_bcv_h1]
                        self.udma[(hb, half)] = sch.add("g", f, waits=w,
                                                        inc=True, inc_n=16)

                def _emit_signu_chunk(self, hb, q):
                    def fs(eng, q=q):
                        reg = Wa[:, q * 2048:(q + 1) * 2048]
                        return eng.sign(reg, reg)
                    self.ev_signu[hb] = sch.add(
                        "a", fs, waits=[self.udma[(hb, q // 2)]], inc=True)

                def _emit_slab_u(self, hb):
                    self.slab_u.add(hb)
                    self._emit_uraw(hb)
                    for q in range(4):
                        self._emit_signu_chunk(hb, q)

                def _emit_slab_d(self, hb):
                    self.slab_d.add(hb)
                    for half in (0, 1):
                        def f(eng, hb=hb, half=half):
                            return eng.dma_start(
                                out=Wb.ap().rearrange("p (c j) -> p c j", j=2048)[:, half * 2:(half + 1) * 2, :],
                                in_=dnt_r[:, hb * 4 + half * 2: hb * 4 + half * 2 + 2, :])
                        w = [self.ev_dng.get(hb - 1)]
                        if hb == 0:
                            w += [ev_aff[31], ev_bcv_h1]
                        self.ddma[(hb, half)] = sch.add("g", f, waits=w,
                                                        inc=True, inc_n=16)

                def _emit_signd_chunk(self, hb, q):
                    def fs(eng, q=q):
                        reg = Wb[:, q * 2048:(q + 1) * 2048]
                        return eng.sign(reg, reg)
                    self.ev_signd[hb] = sch.add(
                        "a", fs, waits=[self.ddma[(hb, q // 2)]], inc=True)

                def emit(self, k, pred=None):
                    c = 0
                    while c < k and self.pos < len(self.seq):
                        it = self.seq[self.pos]
                        if pred is not None and not pred(it):
                            break
                        kind, hb, x, tcx = it
                        n = self.n
                        ff = FF[n % 2]
                        if kind == "u":
                            if hb not in self.slab_u:
                                self._emit_slab_u(hb)
                            if hb not in self.slab_d:
                                self._emit_slab_d(hb)
                            w = [self.ev_signu[hb], self.ev_evac.get(n - 2)]
                            if hb == 0:
                                w.append(ev_aff[15])
                            def f(eng, hc=x, tcx=tcx, ff=ff):
                                for c_ in range(16):
                                    ins = eng.matmul(
                                        ff[:, :],
                                        Wa[:, c_ * 512 + hc * 128: c_ * 512 + hc * 128 + 128],
                                        Cxn[:, c_ * TOWN + tcx * 512: c_ * TOWN + tcx * 512 + 512],
                                        start=(c_ == 0), stop=(c_ == 15))
                                return ins
                            tok = sch.add("t", f, waits=w, inc=True)
                            self.ev_upg[hb] = tok
                            def fr(eng, hb=hb, hc=x, tcx=tcx, ff=ff):
                                return eng.activation(
                                    Hb[:, hc * 1024 + tcx * 512: hc * 1024 + tcx * 512 + 512],
                                    ff[:, :], AF.Relu,
                                    scale=us_sb[:, hb * 4 + hc: hb * 4 + hc + 1])
                            rt = sch.add("a", fr, waits=[tok, self.ev_dng.get(hb - 1)], inc=True)
                            self.ev_relu[(hb, tcx)] = rt
                            self.ev_evac[n] = rt
                            gidx = x * 2 + tcx
                            if 3 <= gidx <= 6:
                                self._emit_signd_chunk(hb, gidx - 3)
                        else:
                            didx = x * 2 + tcx
                            if didx == 0 and hb + 1 < 16 and hb + 1 not in self.slab_u:
                                self.slab_u.add(hb + 1)
                                self._emit_uraw(hb + 1)
                            if didx in (24, 26, 28, 30) and hb + 1 < 16:
                                self._emit_signu_chunk(hb + 1, (didx - 24) // 2)
                            w = [self.ev_signd[hb], self.ev_relu[(hb, tcx)],
                                 self.ev_evac.get(n - 2)]
                            def f(eng, df=x, tcx=tcx, ff=ff):
                                for hc in range(4):
                                    ins = eng.matmul(
                                        ff[:, :],
                                        Wb[:, hc * 2048 + df * 128: hc * 2048 + df * 128 + 128],
                                        Hb[:, hc * 1024 + tcx * 512: hc * 1024 + tcx * 512 + 512],
                                        start=(hc == 0), stop=(hc == 3))
                                return ins
                            tok = sch.add("t", f, waits=w, inc=True)
                            self.ev_dng[hb] = tok
                            def fv(eng, hb=hb, df=x, tcx=tcx, ff=ff):
                                dst = WS[:, df * 1024 + tcx * 512: df * 1024 + tcx * 512 + 512]
                                if hb == 0:
                                    return eng.tensor_copy(dst, ff[:, :])
                                return eng.tensor_tensor(dst, dst, ff[:, :], op=ALU.add)
                            vt = sch.add("v", fv, waits=[tok], inc=True)
                            self.ev_evac[n] = vt
                            self.ev_sacc[(hb, x, tcx)] = vt
                        self.n += 1
                        self.pos += 1
                        c += 1
                    return c

            ffn = FStream()
            ev_bb1 = None

            psq = [R0[:, 0:512], R0[:, 512:1024]]
            psv = [R1[:, 0:512], R1[:, 512:1024]]
            ev_mm_sl = {}
            ev_mm_first = {}
            ev_ev = {}
            ev_vevac = {}
            ev_dma_sl = {}
            ev_kevac_last = None
            qv_i = 0
            kv_i = 0
            for sl in range(6):
                if sl == 1:
                    ev_bb1 = emit_ln_stats(1, BE, PVF[0:1, 0:1024],
                                           PVF[0:1, 1024:2048])
                if sl == 2:
                    ev_bcv_h1 = emit_ln_affine(1, ev_bb1, BE, BE,
                                               PVF[:, 0:512], PVF[:, 512:1024])
                Wb_ = W1 if sl % 2 == 0 else W2
                if sl < 2:
                    ev_dma_sl[sl] = pre_toks[sl]
                else:
                    def f(eng, sl=sl, Wb_=Wb_):
                        return eng.dma_start(
                            out=Wb_.rearrange("p (c j) -> p c j", j=512),
                            in_=wqkv_r[:, :, sl * 512:(sl + 1) * 512])
                    ev_dma_sl[sl] = sch.add("g", f,
                                            waits=[ev_mm_sl.get(sl - 2),
                                                   ev_mm_first.get(sl - 1)],
                                            inc=True, inc_n=16)
                if sl < 4:
                    for fsub in range(4):
                        qf = sl * 4 + fsub
                        for tc in range(2):
                            ps = psq[qv_i % 2]
                            def f(eng, Wb_=Wb_, fsub=fsub, tc=tc, ps=ps):
                                for c in range(16):
                                    ins = eng.matmul(
                                        ps,
                                        Wb_[:, c * 512 + fsub * 128: c * 512 + fsub * 128 + 128],
                                        Cxn[:, c * TOWN + tc * 512: c * TOWN + tc * 512 + 512],
                                        start=(c == 0), stop=(c == 15))
                                return ins
                            ev_mm = sch.add(
                                "t", f,
                                waits=[ev_dma_sl[sl], ev_aff[15],
                                       ev_ev.get(("q", qv_i % 2))],
                                inc=True)
                            ev_mm_first.setdefault(sl, ev_mm)
                            ev_mm_sl[sl] = ev_mm
                            def f(eng, qf=qf, tc=tc, ps=ps):
                                return eng.activation(
                                    A1[:, qf * TOWN + tc * 512: qf * TOWN + tc * 512 + 512],
                                    ps, AF.Identity,
                                    bias=qbias[:, qf:qf + 1], scale=0.125)
                            ev_ev[("q", qv_i % 2)] = sch.add("a", f, waits=[ev_mm], inc=True)
                            qv_i += 1
                elif sl == 4:
                    for fsub in range(4):
                        for tcf in range(4):
                            ps = psq[qv_i % 2]
                            src = Cxn if tcf < 2 else BE
                            tcl = tcf % 2
                            hh = 0 if tcf < 2 else 1
                            def f(eng, Wb_=Wb_, fsub=fsub, src=src, tcl=tcl, ps=ps):
                                for c in range(16):
                                    ins = eng.matmul(
                                        ps,
                                        Wb_[:, c * 512 + fsub * 128: c * 512 + fsub * 128 + 128],
                                        src[:, c * TOWN + tcl * 512: c * TOWN + tcl * 512 + 512],
                                        start=(c == 0), stop=(c == 15))
                                return ins
                            ev_mm = sch.add(
                                "t", f,
                                waits=[ev_dma_sl[sl], ev_aff[hh * 16 + 15],
                                       ev_ev.get(("q", qv_i % 2))],
                                inc=True)
                            ev_mm_first.setdefault(sl, ev_mm)
                            ev_mm_sl[sl] = ev_mm
                            def f(eng, fsub=fsub, tcf=tcf, ps=ps):
                                return eng.activation(
                                    Ksb[:, fsub * S + tcf * 512: fsub * S + tcf * 512 + 512],
                                    ps, AF.Identity,
                                    bias=bq_sb[:, 16 + fsub:17 + fsub], scale=1.0)
                            ev_kevac_last = ev_ev[("q", qv_i % 2)] = \
                                sch.add("a", f, waits=[ev_mm], inc=True)
                            qv_i += 1
                else:
                    for tt in range(16):
                        ps = psv[kv_i % 2]
                        src = Cxn if tt < 8 else BE
                        ttl = tt % 8
                        hh = 0 if tt < 8 else 1
                        def f(eng, Wb_=Wb_, src=src, ttl=ttl, ps=ps):
                            for c in range(16):
                                ins = eng.matmul(
                                    ps,
                                    src[:, c * TOWN + ttl * 128: c * TOWN + ttl * 128 + 128],
                                    Wb_[:, c * 512: c * 512 + 512],
                                    start=(c == 0), stop=(c == 15))
                            return ins
                        ev_mm = sch.add(
                            "t", f,
                            waits=[ev_dma_sl[sl], ev_aff[hh * 16 + 15],
                                   ev_ev.get(("v", kv_i % 2)), ev_vm, ev_bcv_h1],
                            inc=True)
                        ev_mm_first.setdefault(sl, ev_mm)
                        ev_mm_sl[sl] = ev_mm
                        def f(eng, tt=tt, ps=ps):
                            o3 = A2[:, Vaug_off + tt * 520: Vaug_off + tt * 520 + 520]
                            return eng.tensor_tensor(
                                o3.rearrange("p (g e) -> p g e", e=65)[:, :, 0:64],
                                ps.rearrange("p (g e) -> p g e", e=64),
                                vb_bc[:, :].rearrange("p (g e) -> p g e", e=64),
                                op=ALU.add)
                        ev_vevac[tt] = ev_ev[("v", kv_i % 2)] = \
                            sch.add("v", f, waits=[ev_mm], inc=True)
                        kv_i += 1
                        if tt % 2 == 1:
                            ffn.emit(1, pred=lambda it: it[0] == "u" and it[1] == 0)

            ev_qk = {}
            ev_exp = {}
            ev_pv = {}
            ev_pvcopy = {}
            ev_recip = {}
            ev_bct = {}
            ev_bccopy = {}
            ev_mult = {}
            ev_aodma = {}

            def unit_params(P):
                c_ = P // 8
                i_ = (P % 8) // 2
                tc = P % 2
                return c_, i_, tc, 4 * c_ + i_, 2 * c_, 2 * c_ + 1

            def emit_recip(P):
                def frc(eng):
                    eng.activation(pvstage[64:65, 0:1024],
                                   pvstage[64:65, 0:1024], AF.Ln)
                    eng.drain()
                    return eng.activation(recb[64:65, 0:1024],
                                          pvstage[64:65, 0:1024], AF.Exp, scale=-1.0)
                ev_recip[P] = sch.add("a", frc,
                                      waits=[ev_pvcopy[P], ev_bct.get(P - 1)],
                                      inc=True)

            def emit_norm(P):
                def fb(eng):
                    eng.matmul(PV0[0:64, :], onesb2[64:65, 0:64],
                               recb[64:65, 0:512],
                               start=True, stop=True, tile_position=(64, 0))
                    return eng.matmul(PV1[0:64, :], onesb2[64:65, 0:64],
                                      recb[64:65, 512:1024],
                                      start=True, stop=True, tile_position=(64, 0))
                ev_bct[P] = sch.add("t", fb, waits=[ev_recip[P]], inc=True)
                def fc(eng):
                    eng.tensor_copy(bc_sb[0:64, :], PV0[0:64, :])
                    return eng.tensor_copy(bc_sb2[0:64, :], PV1[0:64, :])
                ev_bccopy[P] = sch.add("v", fc, waits=[ev_bct[P]], inc=True)

            def emit_mult(P):
                c_, i_, tc, qch, gA, gB = unit_params(P)
                slot = qch * TOWN + tc * 512
                def fm(eng, P=P, slot=slot):
                    eng.tensor_tensor(A1[0:64, slot:slot + 512],
                                      pvstage[0:64, 0:512], bc_sb[0:64, :],
                                      op=ALU.mult)
                    return eng.tensor_tensor(
                        ao_tmp[0:64, (P % 2) * 512:(P % 2) * 512 + 512],
                        pvstage[0:64, 512:1024], bc_sb2[0:64, :], op=ALU.mult)
                ev_mult[P] = sch.add("v", fm,
                                     waits=[ev_bccopy[P], ev_aodma.get(P - 2)],
                                     inc=True)
                def fd(eng, P=P, slot=slot):
                    return eng.dma_start(
                        out=A1[64:128, slot:slot + 512],
                        in_=ao_tmp[0:64, (P % 2) * 512:(P % 2) * 512 + 512])
                ev_aodma[P] = sch.add("s", fd, waits=[ev_mult[P]],
                                      inc=True, inc_n=16)

            def emit_pv(P, kt):
                c_, i_, tc, qch, gA, gB = unit_params(P)
                slot = (kt % 8) * 1024
                vb0 = Vaug_off + kt * 520 + gA * 65
                vb1 = Vaug_off + kt * 520 + gB * 65
                def fp(eng, kt=kt, slot=slot, vb0=vb0, vb1=vb1):
                    eng.matmul(PV0[0:65, :], A2[:, vb0:vb0 + 65],
                               ering[:, slot:slot + 512],
                               start=(kt == 0), stop=(kt == 15))
                    return eng.matmul(PV1[0:65, :], A2[:, vb1:vb1 + 65],
                                      ering[:, slot + 512:slot + 1024],
                                      start=(kt == 0), stop=(kt == 15))
                w = [ev_exp[(P, kt)]]
                if kt == 0:
                    w += [ev_pvcopy.get(P - 1), ev_bccopy.get(P - 1)]
                if P == 0:
                    w.append(ev_vevac[kt])
                ev_pv[(P, kt)] = sch.add("t", fp, waits=w, inc=True)

            for P in range(32):
                c_, i_, tc, qch, gA, gB = unit_params(P)
                for kt in range(16):
                    reg = regs[kt % 2]
                    def fq(eng, c_=c_, kt=kt, qch=qch, tc=tc, reg=reg):
                        eng.matmul(
                            reg[:, 0:512],
                            Ksb[0:64, c_ * S + kt * 128: c_ * S + kt * 128 + 128],
                            A1[0:64, qch * TOWN + tc * 512: qch * TOWN + tc * 512 + 512],
                            start=True, stop=True)
                        return eng.matmul(
                            reg[:, 512:1024],
                            Ksb[64:128, c_ * S + kt * 128: c_ * S + kt * 128 + 128],
                            A1[64:128, qch * TOWN + tc * 512: qch * TOWN + tc * 512 + 512],
                            start=True, stop=True, tile_position=(64, 0))
                    w = [ev_exp.get((P, kt - 2), ev_exp.get((P - 1, kt + 14)))]
                    if P == 0 and kt == 0:
                        w.append(ev_kevac_last)
                    if P == 0 and kt == 1:
                        w.append(ev_vevac[15])
                    ev_qk[(P, kt)] = sch.add("t", fq, waits=w, inc=True)
                    def fe(eng, kt=kt, reg=reg):
                        return eng.activation(
                            ering[:, (kt % 8) * 1024: (kt % 8) * 1024 + 1024],
                            reg[:, :], AF.Exp, bias=cn20[:, 0:1], scale=1.0)
                    ev_exp[(P, kt)] = sch.add(
                        "a", fe, waits=[ev_qk[(P, kt)],
                                        ev_pv.get((P, kt - 8),
                                                  ev_pv.get((P - 1, kt + 8)))],
                        inc=True)
                    if kt == 0 and P > 0:
                        emit_recip(P - 1)
                    if kt == 2 and P > 0:
                        emit_norm(P - 1)
                    if kt == 3 and P > 0:
                        emit_mult(P - 1)
                    if kt >= 4:
                        emit_pv(P, kt - 4)
                    ffn.emit(1)
                for kt in (12, 13, 14, 15):
                    emit_pv(P, kt)
                    ffn.emit(1)
                def fpc(eng):
                    eng.tensor_copy(pvstage[0:65, 0:512], PV0[0:65, :])
                    return eng.tensor_copy(pvstage[0:65, 512:1024], PV1[0:65, :])
                ev_pvcopy[P] = sch.add("v", fpc,
                                       waits=[ev_pv[(P, 15)], ev_mult.get(P - 1)],
                                       inc=True)
                ffn.emit(2)
            emit_recip(31)
            emit_norm(31)
            emit_mult(31)

            ev_wodma = {}
            ev_womm = {}
            ev_woev_i = {}
            ev_woev = {}
            ev_xt = {}
            ev_pp = {}
            ev_out = {}
            ev_silu = {}
            gt_off = 8192
            xt_off = 10240
            os_off = 14336

            def emit_xt(df):
                def fx(eng, df=df):
                    return eng.dma_start(
                        out=BE[:, xt_off + (df % 2) * 1024: xt_off + (df % 2) * 1024 + 1024],
                        in_=xfull[df * 128:(df + 1) * 128, 0:TOWN])
                ev_xt[df] = sch.add("g", fx, waits=[ev_pp.get(df - 2)],
                                    inc=True, inc_n=16)

            def emit_p6(df):
                def fs(eng, df=df):
                    return eng.activation(
                        BE[:, gt_off + (df % 2) * 1024: gt_off + (df % 2) * 1024 + 1024],
                        WS[:, df * 1024:(df + 1) * 1024], AF.Silu,
                        scale=dsos[:, df:df + 1])
                ev_silu[df] = sch.add("a", fs,
                                      waits=[ffn.ev_sacc[(15, df, 1)],
                                             ev_pp.get(df - 2)], inc=True)
                def fp6(eng, df=df):
                    g = BE[:, gt_off + (df % 2) * 1024: gt_off + (df % 2) * 1024 + 1024]
                    eng.tensor_tensor(g, g, A2[:, df * TOWN:(df + 1) * TOWN],
                                      op=ALU.mult)
                    eng.drain()
                    return eng.tensor_tensor(
                        BE[:, os_off + (df % 2) * 1024: os_off + (df % 2) * 1024 + 1024],
                        BE[:, xt_off + (df % 2) * 1024: xt_off + (df % 2) * 1024 + 1024],
                        g, op=ALU.add)
                ev_pp[df] = sch.add("v", fp6,
                                    waits=[ev_silu[df], ev_woev[(df, 1)],
                                           ev_xt[df], ev_out.get(df - 2)],
                                    inc=True)
                def fo(eng, df=df):
                    return eng.dma_start(
                        out=out[df * 128:(df + 1) * 128, :],
                        in_=BE[:, os_off + (df % 2) * 1024: os_off + (df % 2) * 1024 + 1024])
                ev_out[df] = sch.add("g", fo, waits=[ev_pp[df]],
                                     inc=True, inc_n=16)
                if df + 2 < 16:
                    emit_xt(df + 2)

            def emit_wodma(w_):
                sl, hf = w_ // 2, w_ % 2
                def fw(eng, w_=w_, sl=sl, hf=hf):
                    dst = BE[:, (w_ % 2) * 4096:(w_ % 2 + 1) * 4096]
                    return eng.dma_start(
                        out=dst.rearrange("p (c j) -> p c j", j=256),
                        in_=wo_r[:, :, sl * 512 + hf * 256: sl * 512 + hf * 256 + 256])
                ww = [ev_womm.get(w_ - 2)]
                if w_ == 0:
                    ww.append(ev_pv[(31, 11)])
                if w_ == 1:
                    ww.append(ev_pv[(31, 15)])
                ev_wodma[w_] = sch.add("g", fw, waits=ww, inc=True, inc_n=16)

            emit_wodma(0)
            emit_wodma(1)
            emit_xt(0)
            emit_xt(1)
            wo_i = 0
            for w_ in range(8):
                sl, hf = w_ // 2, w_ % 2
                if w_ >= 1:
                    emit_p6(2 * (w_ - 1))
                    emit_p6(2 * (w_ - 1) + 1)
                for fsub in range(2):
                    ef = sl * 4 + hf * 2 + fsub
                    for tc2 in range(2):
                        ps = psq[wo_i % 2]
                        def fm(eng, w_=w_, fsub=fsub, tc2=tc2, ps=ps):
                            base = (w_ % 2) * 4096
                            for c in range(16):
                                ins = eng.matmul(
                                    ps,
                                    BE[:, base + c * 256 + fsub * 128: base + c * 256 + fsub * 128 + 128],
                                    A1[:, c * TOWN + tc2 * 512: c * TOWN + tc2 * 512 + 512],
                                    start=(c == 0), stop=(c == 15))
                            return ins
                        ww2 = [ev_wodma[w_], ev_woev_i.get(wo_i - 2)]
                        if wo_i == 0:
                            ww2 += [ev_mult[31], ev_aodma[31],
                                    ev_exp[(31, 15)]] + sch.dma_tokens("s")
                        if wo_i == 1:
                            ww2.append(ev_exp[(31, 15)])
                        ev_womm[w_] = sch.add("t", fm, waits=ww2, inc=True)
                        def fe(eng, ef=ef, tc2=tc2, ps=ps):
                            return eng.activation(
                                A2[:, ef * TOWN + tc2 * 512: ef * TOWN + tc2 * 512 + 512],
                                ps, AF.Identity, bias=bo_sb[:, ef:ef + 1], scale=1.0)
                        ev_woev_i[wo_i] = ev_woev[(ef, tc2)] = \
                            sch.add("a", fe, waits=[ev_womm[w_]], inc=True)
                        wo_i += 1
                ffn.emit(2)
                if w_ + 2 < 8:
                    emit_wodma(w_ + 2)
            while ffn.emit(1):
                pass
            emit_p6(14)
            emit_p6(15)
            sch.add("g", lambda eng: eng.nop(), waits=sch.dma_tokens("g"))
            sch.add("s", lambda eng: eng.nop(), waits=sch.dma_tokens("s"))
            sch.run(blk)

    return nc



def _head_perm():
    perm = np.zeros(D, dtype=np.int64)
    for c in range(4):
        for i in range(4):
            for b in range(2):
                h = 8 * c + i + 4 * b
                newbase = (4 * c + i) * 128 + b * 64
                perm[newbase:newbase + 64] = h * 64 + np.arange(64)
    return perm


def _prep_inputs(L0_attn_x, L0_attn_Wqkv, L0_attn_bqkv, L0_attn_Wo, L0_attn_bo,
                 L0_ffn_up_weight, L0_ffn_down_weight, L0_ffn_up_scale,
                 L0_ffn_down_scale, L0_ffn_output_scale):
    perm = _head_perm()
    wq = np.asarray(L0_attn_Wqkv)[:, :D][:, perm]
    wkv = np.asarray(L0_attn_Wqkv)[:, D:]
    wqkv = np.ascontiguousarray(np.concatenate([wq, wkv], axis=1), dtype=np.float32)
    bq = np.asarray(L0_attn_bqkv)[:D][perm]
    bqkv = np.ascontiguousarray(np.concatenate([bq, np.asarray(L0_attn_bqkv)[D:]]),
                                dtype=np.float32)
    wo = np.ascontiguousarray(np.asarray(L0_attn_Wo)[perm, :], dtype=np.float32)
    upt = np.ascontiguousarray(np.asarray(L0_ffn_up_weight).T, dtype=np.float32)
    dnt = np.ascontiguousarray(np.asarray(L0_ffn_down_weight).T, dtype=np.float32)
    xT = np.ascontiguousarray(np.asarray(L0_attn_x).transpose(0, 2, 1), dtype=np.float32)

    common = {
        "wqkv": wqkv, "bqkv": bqkv, "wo": wo,
        "bo": np.ascontiguousarray(np.asarray(L0_attn_bo), dtype=np.float32),
        "upt": upt, "dnt": dnt,
        "us": np.ascontiguousarray(np.asarray(L0_ffn_up_scale), dtype=np.float32),
        "ds": np.ascontiguousarray(np.asarray(L0_ffn_down_scale), dtype=np.float32),
        "osc": np.ascontiguousarray(np.asarray(L0_ffn_output_scale), dtype=np.float32),
    }
    in_maps = []
    for core in range(8):
        b, half = core // 2, core % 2
        xb = xT[b]
        if half == 0:
            xc = xb
        else:
            xc = np.concatenate([xb[:, TOWN:], xb[:, :TOWN]], axis=1)
        m = dict(common)
        m["x"] = np.ascontiguousarray(xc)
        in_maps.append(m)
    return in_maps


def kernel(**inputs):
    in_maps = _prep_inputs(**inputs)
    nc = build_program()
    res = run_bass_kernel_spmd(nc, in_maps, core_ids=list(range(8)))
    outT = np.zeros((4, D, S), dtype=np.float32)
    for core in range(8):
        b, half = core // 2, core % 2
        outT[b][:, half * TOWN:(half + 1) * TOWN] = res.results[core]["out"]
    return np.ascontiguousarray(outT.transpose(0, 2, 1))
